# version 13
# baseline (speedup 1.0000x reference)
import numpy as np
import concourse.bass as bass
import concourse.mybir as mybir
from concourse.bass_utils import run_bass_kernel_spmd
from concourse.replica_groups import maybe_share_collective_output_space
from contextlib import ExitStack

F32 = mybir.dt.float32
BF16 = mybir.dt.bfloat16
FP8 = mybir.dt.float8e4
AF = mybir.ActivationFunctionType
ALU = mybir.AluOpType
AX = mybir.AxisListType

D = 2048
G = 16
CH = 512
TPC = 4
NCORE = 8
INV_D = 1.0 / D
EPS = 1e-5
ENGS = ("sync", "scalar", "vector", "tensor", "gpsimd")
SEM_OWNER = {"sV": "vector", "sS": "scalar", "sT": "tensor", "sG": "gpsimd"}
FAMS = ("in", "dec", "gate")
FAM_AF = {"in": AF.Silu, "dec": AF.Sigmoid, "gate": AF.Sigmoid}
FAM_BIAS = {"in": "b_in", "dec": "bdd", "gate": "b_gate"}


class Plan:
    def __init__(self, nc):
        self.nc = nc
        self.ops = {e: [] for e in ENGS}
        self.hs = {}
        self.cnt = {}
        self.last_wait = {}
        self.trace = []

    def new_sem(self, stack, name):
        self.hs[name] = stack.enter_context(self.nc.semaphore(name))
        self.cnt[name] = 0

    def wait(self, eng, name, val):
        if val is None or val <= 0:
            return
        own = SEM_OWNER.get(name)
        if own == eng:
            assert val <= self.cnt[name], f"own-engine future wait {eng} {name} {val}"
            return
        key = (eng, name)
        if self.last_wait.get(key, 0) >= val:
            return
        self.last_wait[key] = val
        self.trace.append((eng, "wait", name, val))
        h = self.hs[name]
        self.ops[eng].append(lambda e, h=h, v=val: e.wait_ge(h, v))

    def op(self, eng, build, inc=None, incn=1):
        import traceback
        ln = traceback.extract_stack()[-2].lineno
        self.trace.append((eng, "op", inc, self.cnt.get(inc, 0) + incn if inc else None, ln))
        if inc is not None:
            self.cnt[inc] += incn
            after = self.cnt[inc]
            h = self.hs[inc]
            self.ops[eng].append(lambda e, b=build, h=h, n=incn: b(e).then_inc(h, n))
            return after
        self.ops[eng].append(lambda e, b=build: b(e))
        return None

    def run(self, eng_name, eng):
        for fn in self.ops[eng_name]:
            fn(eng)


class Ring:
    def __init__(self, tiles):
        self.tiles = tiles
        self.n = len(tiles)
        self.free = [[] for _ in range(self.n)]

    def acquire(self, P, eng, idx):
        s = idx % self.n
        for sem, val in self.free[s]:
            P.wait(eng, sem, val)
        self.free[s] = []
        return self.tiles[s]

    def set_free(self, idx, conds):
        self.free[idx % self.n] = [c for c in conds if c[1]]


def build_program(nchunk=8, n_ranks=NCORE, dbg=False, phase1_only=False, skip_c=False, xnt_input=False):
    nc = bass.Bass("TRN2", target_bir_lowering=False)
    S = nchunk * CH

    x_d = nc.dram_tensor("x", [S, D], F32, kind="ExternalInput").ap()
    fam_src = {}
    for f, nm in (("in", "w_in"), ("dec", "w_dec"), ("gate", "w_gate"), ("out", "w_out")):
        fam_src[f] = nc.dram_tensor(nm, [D, D], F32, kind="ExternalInput").ap()
    ffnw_d = nc.dram_tensor("ffnw", [D, D], F32, kind="ExternalInput").ap()
    VNAMES = ["b_in", "b_dec", "b_gate", "b_out", "dbias", "gamma", "beta", "ffnb", "alpha"]
    vecs_d = {vn: nc.dram_tensor(vn, [D], F32, kind="ExternalInput").ap() for vn in VNAMES}
    ident_d = nc.dram_tensor("ident", [128, 128], F32, kind="ExternalInput").ap()
    xnt_d = (nc.dram_tensor("xnt_in", [nchunk, G * 128, CH], BF16, kind="ExternalInput").ap()
             if xnt_input else None)
    selm_d = nc.dram_tensor("selm", [128, 8], F32, kind="ExternalInput").ap()
    out_d = nc.dram_tensor("out", [S, D], F32, kind="ExternalOutput").ap()

    kd = "ExternalOutput" if dbg else "Internal"
    wsc = {f: nc.dram_tensor(f"wsc_{f}", [4, 8, 128, 1024], BF16, kind=kd) for f in ("in", "dec", "gate", "out")}
    swsc = nc.dram_tensor("swsc", [4, 8, 128, 1024], FP8, kind=kd)
    sca = nc.dram_tensor("sca", [G, 128, S], F32, kind=kd)
    scat = nc.dram_tensor("scat", [G, 128, S], FP8, kind=kd)
    scb = nc.dram_tensor("scb", [G, 128, S], F32, kind=kd)
    ag_space = maybe_share_collective_output_space(
        "AllGather", [list(range(n_ranks))], prefer_local=True)
    ag_in = nc.dram_tensor("ag_in", [128, 32], F32)
    if dbg:
        dbg_t = {
            "dXN": nc.dram_tensor("dXN", [S, D], F32, kind="ExternalOutput").ap(),
            "dQ": nc.dram_tensor("dQ", [S, D], F32, kind="ExternalOutput").ap(),
            "dXNT": nc.dram_tensor("dXNT", [G, 128, S], BF16, kind="ExternalOutput").ap(),
            "dAT": nc.dram_tensor("dAT", [G, 128, S], FP8, kind="ExternalOutput").ap(),
            "dI": nc.dram_tensor("dI", [G, 128, S], F32, kind="ExternalOutput").ap(),
            "dD": nc.dram_tensor("dD", [G, 128, S], F32, kind="ExternalOutput").ap(),
            "dG": nc.dram_tensor("dG", [G, 128, S], F32, kind="ExternalOutput").ap(),
            "dL": nc.dram_tensor("dL", [G, 128, S], F32, kind="ExternalOutput").ap(),
            "dP": nc.dram_tensor("dP", [G, 128, S], F32, kind="ExternalOutput").ap(),
            "dST": nc.dram_tensor("dST", [nchunk * TPC, 128, 48], F32, kind="ExternalOutput").ap(),
            "dPV": nc.dram_tensor("dPV", [128, 16 * 11], F32, kind="ExternalOutput").ap(),
            "dAS": nc.dram_tensor("dAS", [G, 128, S], F32, kind="ExternalOutput").ap(),
            "dSWF": nc.dram_tensor("dSWF", [G, 4, 128, 512], F32, kind="ExternalOutput").ap(),
            "dRM": nc.dram_tensor("dRM", [G, 128, 8], F32, kind="ExternalOutput").ap(),
        }
    ag_out = nc.dram_tensor("ag_out", [128 * n_ranks, 32], F32, addr_space=ag_space)

    stack = ExitStack()
    sb = lambda name, shape, dt: stack.enter_context(nc.sbuf_tensor(name, shape, dt))

    ident = sb("identsb", [128, 128], F32)
    SLOT = {n: i for i, n in enumerate(VNAMES + ["bdd", "alphabias"])}
    pvec = sb("pvec", [128, 16 * len(SLOT)], F32)
    selm = sb("selmsb", [128, 8], F32)
    epsc = sb("epsc", [128, 1], F32)
    zeros = sb("zerosc", [128, CH], F32)
    bnd = sb("bnd", [128, 32], F32)
    gath = sb("gath", [128, 32 * n_ranks], F32)
    sinP = [sb(f"sin{i}", [128, 16], F32) for i in range(2)]
    sinT = [sb(f"sint{i}", [128, 16], F32) for i in range(2)]
    sinF = sb("sinF", [128, 16], F32)
    statc = sb("statc", [128, 48 * TPC], F32)

    xr = Ring([sb(f"xr{i}", [128, D], F32) for i in range(2)])
    xnr = Ring([sb(f"xnr{i}", [128, D], F32) for i in range(4)])
    qr = Ring([sb(f"qr{i}", [128, D], F32) for i in range(2)])
    xnT = sb("xnT", [128, G * CH], BF16)
    aT = sb("aT", [128, G * CH], FP8)
    rI = Ring([sb(f"ii{i}", [128, CH], F32) for i in range(4)])
    rD = Ring([sb(f"dd{i}", [128, CH], F32) for i in range(4)])
    rG = Ring([sb(f"gg{i}", [128, CH], F32) for i in range(4)])
    rU = Ring([sb(f"uu{i}", [128, CH], F32) for i in range(4)])
    rL = Ring([sb(f"ll{i}", [128, CH], F32) for i in range(4)])
    rP = Ring([sb(f"pp{i}", [128, CH], F32) for i in range(4)])
    rAs = Ring([sb(f"ast{i}", [128, CH], F32) for i in range(2)])
    rBs = Ring([sb(f"bst{i}", [128, CH], F32) for i in range(2)])
    rGs = Ring([sb(f"gst{i}", [128, CH], F32) for i in range(2)])
    wring = {f: Ring([sb(f"w_{f}_{i}", [128, 2 * CH], BF16) for i in range(2)])
             for f in ("in", "dec", "gate", "out")}
    swring = Ring([sb(f"swr{i}", [128, 2 * CH], FP8) for i in range(2)])
    rA2, rB2, rGl, rX2, rOut, rSs = rAs, rBs, rGs, xr, xnr, rI

    prostack = ExitStack()
    psb = lambda name, shape, dt: prostack.enter_context(nc.sbuf_tensor(name, shape, dt))
    convfV = [psb(f"convfV{i}", [128, 512], F32) for i in range(4)]
    convbV = [psb(f"convbV{i}", [128, 512], BF16) for i in range(4)]
    convfS = [psb(f"convfS{i}", [128, 512], F32) for i in range(2)]
    convbS = [psb(f"convbS{i}", [128, 512], BF16) for i in range(2)]
    sw8 = [psb(f"sw8_{i}", [128, D], FP8) for i in range(2)]
    swf = [psb(f"swf{i}", [128, 512], F32) for i in range(2)]
    swin = [psb("swin0", [128, D], F32)]
    rowm = psb("rowm", [128, 8], F32)
    prostack.close()
    zt = sb("zt", [128, G * CH], F32)

    pb = [stack.enter_context(nc.psum_tensor(f"pb{i}", [128, 512], F32)) for i in range(8)]
    bank_free = [[] for _ in range(8)]

    P = Plan(nc)
    sem_names = ["dPro", "dAgIn", "dAgOut", "dOut0", "dOut1", "dDbg",
                 "sV", "sS", "sT", "sG", "sCC"]
    sem_names += [f"dX{i}" for i in range(3)]
    for f in ("in", "dec", "gate", "out"):
        sem_names += [f"dW_{f}{i}" for i in range(4)]
        sem_names.append(f"dCv_{f}")
    sem_names += [f"dWsw{i}" for i in range(4)] + ["dCv_sw"]
    sem_names += [f"dSpA{i}" for i in range(4)] + [f"dSpB{i}" for i in range(4)]
    sem_names += [f"dSpC{i}" for i in range(4)] + [f"dSpG{i}" for i in range(2)]
    sem_names += [f"dPA{i}" for i in range(6)] + [f"dPB{i}" for i in range(6)]
    sem_names += [f"dPC{i}" for i in range(6)] + [f"dPG{i}" for i in range(2)]
    sem_names += ["dPLV0", "dPLV1", "dPLV2", "dPLV3", "dPLS0", "dPLS1", "dSwin0", "dSwin1"]
    sem_names += ["dSpT0", "dSpT1", "dPT0", "dPT1"]
    for s_ in sem_names:
        P.new_sem(stack, s_)

    def bank_acq(i):
        for sem, val in bank_free[i]:
            P.wait("tensor", sem, val)
        bank_free[i] = []
        return pb[i]

    def bank_rel(i, sem, val):
        bank_free[i] = [(sem, val)]

    def dbg_dump(wait_sem, wait_val, src_ap, dst_ap):
        P.wait("gpsimd", wait_sem, wait_val)
        v = P.op("gpsimd", lambda e, d_=dst_ap, s_=src_ap: e.dma_start(
            out=d_, in_=s_), "dDbg", 16)
        return ("dDbg", v)

    def pcol(slot, g):
        c = SLOT[slot] * 16 + g
        return pvec[:, c:c + 1]

    stat_cols = {"meanv": 32, "varv": 33, "mean1": 34, "lnv1": 35, "rs1": 36,
                 "mu2v": 37, "var2v": 38, "mu2": 39, "lnv2": 40, "rs2": 41,
                 "tmp0": 42, "tmp1": 43, "tmp2": 44, "msq": 45}

    def scol(name, t):
        c = t * 48 + stat_cols[name]
        return statc[:, c:c + 1]

    def bn6(t):
        return statc[:, t * 48:t * 48 + 24]

    def mvc(t, i):
        c = t * 48 + 24 + i
        return statc[:, c:c + 1]

    def emit_bn_stats(xt, t, mean_name, var_name):
        assert stat_cols[var_name] == stat_cols[mean_name] + 1
        for b in range(4):
            P.op("vector", lambda e, x_=xt[:, b * 512:(b + 1) * 512],
                 o_=bn6(t)[:, b * 6:(b + 1) * 6]: e.bn_stats(out=o_, in_=x_))
        for k in range(6):
            P.op("vector", lambda e, o_=scol("tmp0", t): e.tensor_copy(
                out=o_, in_=epsc[:]))
        return P.op("vector", lambda e, i_=bn6(t),
                    o_=statc[:, t * 48 + stat_cols[mean_name]:t * 48 + stat_cols[mean_name] + 2]:
                    e.bn_aggr(out=o_, in_=i_), "sV")

    P.op("sync", lambda e: e.dma_start(out=ident[:], in_=ident_d[:, :]), "dPro", 16)
    P.op("sync", lambda e: e.dma_start(out=selm[:], in_=selm_d[:, :]), "dPro", 16)
    for vn in VNAMES:
        dst = pvec[:, SLOT[vn] * 16:SLOT[vn] * 16 + 16]
        src = vecs_d[vn].rearrange("(g p) -> p g", p=128)
        P.op("sync", lambda e, d_=dst, s_=src: e.dma_start(
            out=d_, in_=s_, allow_slow_non_contiguous=True), "dPro", 16)
    ev_pro_loads = P.cnt["dPro"]

    P.op("gpsimd", lambda e: e.memset(epsc[:], EPS))
    P.op("gpsimd", lambda e: e.memset(zeros[:], 0.0))
    P.op("gpsimd", lambda e: e.memset(bnd[:, 0:16], 0.0))
    P.op("gpsimd", lambda e: e.memset(bnd[:, 16:32], 1.0))
    ev_consts = P.op("gpsimd", lambda e: e.memset(sinP[0][:], 0.0), "sG")

    P.wait("vector", "dPro", ev_pro_loads)
    P.op("vector", lambda e: e.tensor_tensor(
        out=pvec[:, SLOT["bdd"] * 16:SLOT["bdd"] * 16 + 16],
        in0=pvec[:, SLOT["b_dec"] * 16:SLOT["b_dec"] * 16 + 16],
        in1=pvec[:, SLOT["dbias"] * 16:SLOT["dbias"] * 16 + 16], op=ALU.add))
    ev_pvec = P.op("vector", lambda e: e.tensor_tensor(
        out=pvec[:, SLOT["alphabias"] * 16:SLOT["alphabias"] * 16 + 16],
        in0=pvec[:, SLOT["alpha"] * 16:SLOT["alpha"] * 16 + 16],
        in1=pvec[:, SLOT["ffnb"] * 16:SLOT["ffnb"] * 16 + 16], op=ALU.mult), "sV")

    conv_ring = {"vector": Ring(list(zip(convfV, convbV))),
                 "scalar": Ring(list(zip(convfS, convbS)))}
    conv_nslot = {"vector": 4, "scalar": 2}
    conv_idx = {"vector": 0, "scalar": 0}
    ev_conv = {}

    def convert_tile(fam, eng, kt, h):
        esem = "sV" if eng == "vector" else "sS"
        i = conv_idx[eng]
        conv_idx[eng] += 1
        cf, cb = conv_ring[eng].acquire(P, "sync", i)
        lsem = ("dPLV" + str(i % 2)) if eng == "vector" else ("dPLS" + str(i % 2))
        ldv = None
        for hp in range(2):
            src = fam_src[fam][kt * 128 + hp * 64:kt * 128 + (hp + 1) * 64,
                               h * 512:(h + 1) * 512]
            ldv = P.op("sync", lambda e, d_=cf[hp * 64:(hp + 1) * 64, :], s_=src:
                       e.dma_start(out=d_, in_=s_), lsem, 16)
        P.wait(eng, lsem, ldv)
        if eng == "vector":
            cv = P.op(eng, lambda e, o=cb[:], i_=cf[:]: e.tensor_copy(
                out=o, in_=i_), esem)
        else:
            cv = P.op(eng, lambda e, o=cb[:], i_=cf[:]: e.activation(
                out=o, in_=i_, func=AF.Identity), esem)
        P.wait("gpsimd", esem, cv)
        dst = wsc[fam].ap()[h, kt // 2, :, (kt % 2) * 512:(kt % 2) * 512 + 512]
        stv = P.op("gpsimd", lambda e, d_=dst, s_=cb[:]: e.dma_start(
            out=d_, in_=s_), "dCv_" + fam, 16)
        conv_ring[eng].set_free(i, [(esem, cv), ("dCv_" + fam, stv)])

    def convert_fam(fam, eng, kts=None):
        for kt in (range(G) if kts is None else kts):
            for h in range(4):
                convert_tile(fam, eng, kt, h)
        ev_conv[fam] = P.cnt["dCv_" + fam]

    _swg = {"swf_free": [[], []], "sw8_free": [[], []], "swin_free": [[], []]}

    def emit_swgen_ot(ot):
        swf_free = _swg["swf_free"]
        sw8_free = _swg["sw8_free"]
        swin_free = _swg["swin_free"]
        if True:
            slot = ot % 2
            sl2 = 0
            for sem, val in swin_free[sl2]:
                P.wait("sync", sem, val)
            swin_free[sl2] = []
            ldv = P.op("sync", lambda e, d_=swin[sl2][:], s_=ffnw_d[ot * 128:(ot + 1) * 128, :]:
                       e.dma_start(out=d_, in_=s_), "dSwin" + str(sl2), 16)
            P.wait("vector", "dSwin" + str(sl2), ldv)
            rsv = P.op("vector", lambda e, s_=swin[sl2][:]: e.reduce_sum(
                out=rowm[:, 4 * (ot % 2):4 * (ot % 2) + 1], in_=s_, axis=AX.X), "sV")
            P.wait("scalar", "sV", rsv)
            nmS = P.op("scalar", lambda e: e.activation(out=rowm[:, 4 * (ot % 2) + 1:4 * (ot % 2) + 2], in_=rowm[:, 4 * (ot % 2):4 * (ot % 2) + 1],
                                                        func=AF.Identity, scale=-INV_D), "sS")
            P.wait("vector", "sS", nmS)
            nmV = P.op("vector", lambda e: e.tensor_copy(out=rowm[:, 4 * (ot % 2) + 2:4 * (ot % 2) + 3], in_=rowm[:, 4 * (ot % 2) + 1:4 * (ot % 2) + 2]), "sV")
            P.wait("scalar", "sV", nmV)
            for sem, val in sw8_free[slot]:
                P.wait("vector", sem, val)
            sw8_free[slot] = []
            sgs = []
            for h in range(4):
                for sem, val in swf_free[h % 2]:
                    P.wait("scalar", sem, val)
                swf_free[h % 2] = []
                sg = P.op("scalar", lambda e, o_=swf[h % 2][:],
                          i_=swin[sl2][:, h * 512:(h + 1) * 512]: e.activation(
                              out=o_, in_=i_, func=AF.Sign, bias=rowm[:, 4 * (ot % 2) + 2:4 * (ot % 2) + 3]), "sS")
                sgs.append(sg)
                if dbg:
                    dbg_dump("sS", sg, swf[h % 2][:], dbg_t["dSWF"][ot, h, :, :])
                    dbg_dump("sS", sg, rowm[:, 0:8], dbg_t["dRM"][ot, :, :])
                P.wait("tensor", "sS", sg)
                tvs = []
                for it in range(4):
                    b = (h * 4 + it) % 8
                    bt = bank_acq(b)
                    tv = P.op("tensor", lambda e, bk=bt, s_=swf[h % 2][:, it * 128:(it + 1) * 128]:
                              e.transpose(out=bk[:, 0:128], in_=s_, identity=ident[:]), "sT")
                    P.wait("vector", "sT", tv)
                    itg = h * 4 + it
                    jcol = (itg % 2) * 8 + itg // 2
                    ev = P.op("vector", lambda e, bk=bt, o_=sw8[slot][:, jcol * 128:(jcol + 1) * 128]:
                              e.tensor_copy(out=o_, in_=bk[:, 0:128]), "sV")
                    bank_rel(b, "sV", ev)
                    tvs.append(tv)
                swf_free[h % 2] = [("sT", tvs[-1])]
                if dbg:
                    swf_free[h % 2].append(("dDbg", P.cnt["dDbg"]))
            last_ev = P.cnt["sV"]
            swin_free[sl2] = [("sS", sgs[-1]), ("sV", rsv)]
            P.wait("gpsimd", "sV", last_ev)
            stv = None
            oq_, oc_ = ot // 4, (ot % 4) * 128
            for half in range(2):
                for h2 in range(2):
                    dst = swsc.ap()[oq_, half * 4:(half + 1) * 4, :,
                                    h2 * 512 + oc_:h2 * 512 + oc_ + 128]
                    dst = dst.rearrange("it p o -> p it o")
                    lo_ = (h2 * 8 + half * 4) * 128
                    src = sw8[slot][:, lo_:lo_ + 512].rearrange(
                        "p (it o) -> p it o", it=4)
                    stv = P.op("gpsimd", lambda e, d_=dst, s_=src: e.dma_start(
                        out=d_, in_=s_), "dCv_sw", 16)
            sw8_free[slot] = [("dCv_sw", stv)]
            if ot == G - 1:
                ev_conv["sw"] = P.cnt["dCv_sw"]


    ev_xload = {}; ev_xn = {}; ev_q = {}; ev_r2 = {}; ev_mu2 = {}
    ev_A = {}; ev_Aev = {}; ev_B = {}; ev_sign = {}
    ev_C = {}; ev_gffn = {}; ev_D = {}; ev_evac = {}
    ev_u = {}; ev_scan = {}; ev_ab = {}; ev_carry = {}
    ev_spill = {}; ev_wtile = {}

    P.wait("scalar", "sG", ev_consts)
    P.wait("vector", "sG", ev_consts)
    P.wait("scalar", "dPro", ev_pro_loads)
    P.wait("scalar", "sV", ev_pvec)
    if dbg:
        dbg_dump("sV", ev_pvec, pvec[:, 0:16 * 11], dbg_t["dPV"][:, :])
    dbg_xn_conds = {}; dbg_q_conds = {}; dbg_idg_conds = {}

    NXT = nchunk * TPC

    def emit_xload(i):
        if i < 0 or i >= NXT:
            return
        xt = xr.acquire(P, "sync", i)
        lsem = f"dX{i % 2}"
        lv = P.op("sync", lambda e, d_=xt[:], s_=x_d[i * 128:(i + 1) * 128, :]:
                  e.dma_start(out=d_, in_=s_), lsem, 16)
        ev_xload[i] = (lsem, lv, xt)

    emit_xload(0)
    emit_xload(1)

    def emit_stats(c, only_t=None):
        for t in (range(TPC) if only_t is None else [only_t]):
            idx = c * TPC + t
            lsem, lv, xt = ev_xload[idx]
            P.wait("vector", lsem, lv)
            if c > 0:
                P.wait("vector", "sS", ev_r2[(c - 1, t)])
            v1 = emit_bn_stats(xt, t, "meanv", "varv")
            P.wait("scalar", "sV", v1)
            m1 = P.op("scalar", lambda e, i_=scol("meanv", t), o_=scol("mean1", t):
                      e.activation(out=o_, in_=i_, func=AF.Identity), "sS")
            P.op("scalar", lambda e, i_=scol("varv", t), o_=scol("lnv1", t):
                 e.activation(out=o_, in_=i_, func=AF.Ln, bias=epsc[:]))
            r1 = P.op("scalar", lambda e, i_=scol("lnv1", t), o_=scol("rs1", t):
                      e.activation(out=o_, in_=i_, func=AF.Exp, scale=-0.5), "sS")
            xnt = xnr.acquire(P, "vector", idx)
            P.wait("vector", "sS", r1)
            xnv = P.op("vector", lambda e, x_=xt[:], o_=xnt[:], m_=scol("mean1", t),
                       r_=scol("rs1", t): e.tensor_scalar(
                           out=o_, in0=x_, scalar1=m_, scalar2=r_,
                           op0=ALU.subtract, op1=ALU.mult), "sV")
            ev_xn[(c, t)] = (xnv, xnt)
            xr.set_free(idx, [("sV", xnv)])
            emit_xload(idx + 2)
            v2 = emit_bn_stats(xnt, t, "mu2v", "var2v")
            P.wait("scalar", "sV", v2)
            m2 = P.op("scalar", lambda e, i_=scol("mu2v", t), o_=scol("mu2", t):
                      e.activation(out=o_, in_=i_, func=AF.Identity), "sS")
            P.op("scalar", lambda e, i_=scol("var2v", t), o_=scol("lnv2", t):
                 e.activation(out=o_, in_=i_, func=AF.Ln, bias=epsc[:]))
            r2 = P.op("scalar", lambda e, i_=scol("lnv2", t), o_=scol("rs2", t):
                      e.activation(out=o_, in_=i_, func=AF.Exp, scale=-0.5), "sS")
            ev_r2[(c, t)] = r2
            ev_mu2[(c, t)] = max(m2, r2)
            if dbg:
                dbg_dump("sS", r2, statc[:, t * 48:(t + 1) * 48],
                         dbg_t["dST"][c * TPC + t, :, :])
                dbg_xn_conds[(c, t)] = dbg_dump(
                    "sV", xnv, xnt[:], dbg_t["dXN"][idx * 128:(idx + 1) * 128, :])


    emit_stats(0)

    for _kt in range(G):
        for _h in range(4):
            convert_tile("in", "vector", _kt, _h)
            convert_tile("dec", "scalar", _kt, _h)
            convert_tile("gate", "vector", _kt, _h)
    for _f in ("in", "dec", "gate"):
        ev_conv[_f] = P.cnt["dCv_" + _f]

    for c in range(nchunk):
        if xnt_input:
            if c > 0:
                P.wait("sync", "sT", ev_D[(c - 1, 3, "gate")])
            lv = P.op("sync", lambda e, d_=xnT[:].rearrange("p (k s) -> p k s", k=G),
                      s_=xnt_d[c].rearrange("(k p) s -> p k s", p=128):
                      e.dma_start(out=d_, in_=s_), "dAgIn", 16)
            fake = P.op("vector", lambda e, v=lv: e.nop(), "sV")
            P.wait("vector", "dAgIn", lv)
            for t in range(TPC):
                ev_Aev[(c, t, 3)] = fake
                ev_A[(c, t, 3)] = P.cnt["sT"]
                ev_B[(c, t, 3)] = P.cnt["sT"]
                xnv, xnt = ev_xn[(c, t)]
                xnr.set_free(c * TPC + t, [("sV", fake)] +
                             ([dbg_xn_conds[(c, t)]] if dbg else []))
        if (not xnt_input) and c > 0:
            for sem2, val2 in ev_at_free_prev:
                P.wait("scalar", sem2, val2)
        for t in range(TPC) if not xnt_input else []:
            xnv, xnt = ev_xn[(c, t)]
            qt = qr.acquire(P, "vector", c * TPC + t)
            P.wait("vector", "sS", ev_mu2[(c, t)])
            qv = P.op("vector", lambda e, x_=xnt[:], o_=qt[:], m_=scol("mu2", t),
                      r_=scol("rs2", t): e.tensor_scalar(
                          out=o_, in0=x_, scalar1=m_, scalar2=r_,
                          op0=ALU.subtract, op1=ALU.mult), "sV")
            ev_q[(c, t)] = (qv, qt)
            if dbg:
                idx = c * TPC + t
                dbg_q_conds[(c, t)] = dbg_dump(
                    "sV", qv, qt[:], dbg_t["dQ"][idx * 128:(idx + 1) * 128, :])
            if c + 1 < nchunk:
                emit_stats(c + 1, only_t=t)
            P.wait("tensor", "sV", xnv)
            loA = (t % 2) * 4
            for j in range(4):
                b = loA + j
                bt = bank_acq(b)
                tv = None
                for qq in range(4):
                    g = 4 * j + qq
                    tv = P.op("tensor", lambda e, bk=bt, s_=xnt[:, g * 128:(g + 1) * 128],
                              q_=qq: e.transpose(out=bk[:, q_ * 128:(q_ + 1) * 128],
                                                 in_=s_, identity=ident[:]),
                              "sT" if qq == 3 else None)
                ev_A[(c, t, j)] = tv
                P.wait("scalar", "sT", tv)
                if c > 0 and t == 0 and j == 0:
                    P.wait("scalar", "sT", ev_D[(c - 1, 3, "gate")])
                o_ap = xnT[:].rearrange("p (g s) -> p g s", g=G)[
                    :, 4 * j:4 * j + 4, t * 128:(t + 1) * 128]
                last = P.op("scalar", lambda e, bk=bt, o_=o_ap:
                            e.activation(out=o_, in_=bk[:, 0:512].rearrange(
                                "p (q s) -> p q s", q=4), func=AF.Identity), "sS")
                ev_Aev[(c, t, j)] = last
                bank_rel(b, "sS", last)
            P.wait("tensor", "sV", qv)
            loB = ((t + 1) % 2) * 4
            for j in range(4):
                b = loB + j
                bt = bank_acq(b)
                tv = None
                for qq in range(4):
                    g = 4 * j + qq
                    tv = P.op("tensor", lambda e, bk=bt, s_=qt[:, g * 128:(g + 1) * 128],
                              q_=qq: e.transpose(out=bk[:, q_ * 128:(q_ + 1) * 128],
                                                 in_=s_, identity=ident[:]),
                              "sT" if qq == 3 else None)
                ev_B[(c, t, j)] = tv
                P.wait("scalar", "sT", tv)
                last = None
                for qq in range(4):
                    g = 4 * j + qq
                    last = P.op("scalar", lambda e, bk=bt, q_=qq, g_=g,
                                o_=aT[:, g * CH + t * 128:g * CH + (t + 1) * 128]:
                                e.activation(out=o_, in_=bk[:, q_ * 128:(q_ + 1) * 128],
                                             func=AF.Sign, scale=pcol("gamma", g_),
                                             bias=pcol("beta", g_)),
                                "sS" if qq == 3 else None)
                ev_sign[(c, t, j)] = last
                bank_rel(b, "sS", last)
            xn_conds = [("sT", ev_A[(c, t, 3)])]
            q_conds = [("sT", ev_B[(c, t, 3)])]
            if dbg:
                xn_conds.append(dbg_xn_conds[(c, t)])
                q_conds.append(dbg_q_conds[(c, t)])
            xnr.set_free(c * TPC + t, xn_conds)
            qr.set_free(c * TPC + t, q_conds)

        ev_at_free_prev = None
        ev_sign_all = P.cnt["sS"]
        for kt2 in range(2):
            st_sem = f"dSpT{(c * 2 + kt2) % 2}"
            P.wait("gpsimd", "sS", ev_sign[(c, TPC - 1, 3)])
            spv = None
            for hp in range(2):
                dst = scat.ap()[kt2 * 8:(kt2 + 1) * 8, hp * 64:(hp + 1) * 64,
                                c * CH:(c + 1) * CH]
                dst = dst.rearrange("g p s -> p g s")
                spv = P.op("gpsimd", lambda e, d_=dst,
                           s_=aT[hp * 64:(hp + 1) * 64,
                                 kt2 * 8 * CH:(kt2 + 1) * 8 * CH].rearrange(
                               "p (g s) -> p g s", g=8): e.dma_start(out=d_, in_=s_),
                           st_sem, 16)
            ev_spill[("at", c, kt2)] = (st_sem, spv)
        ev_at_free_prev = [ev_spill[("at", c, 0)], ev_spill[("at", c, 1)]]

        for ot_ in range(c * 3, min(c * 3 + 3, G)):
            emit_swgen_ot(ot_)
        if 1 <= c <= 4:
            convert_fam("out", "scalar", kts=range((c - 1) * 4, c * 4))

        def emit_wload(fam, j, c=c):
            if j >= 32:
                return
            mq_, kp = j // 8, j % 8
            if c == 0 and j == 0:
                P.wait("sync", "dCv_" + fam, ev_conv[fam])
            wt = wring[fam].acquire(P, "sync", c * 32 + j)
            lsem = f"dW_{fam}{(c * 32 + j) % 2}"
            wv = None
            for hp in range(2):
                src = wsc[fam].ap()[mq_, kp, hp * 64:(hp + 1) * 64, :]
                wv = P.op("sync", lambda e, d_=wt[hp * 64:(hp + 1) * 64, :], s_=src:
                          e.dma_start(out=d_, in_=s_), lsem, 16)
            for kk in range(2):
                ev_wtile[(fam, c, mq_, kp * 2 + kk)] = (lsem, wv, wt, c * 32 + j, kk)

        for fam in FAMS:
            for j in range(2):
                emit_wload(fam, j)
        P.wait("tensor", "sS", ev_Aev[(c, TPC - 1, 3)])
        grpctr = 0
        for mq in range(4):
            for fam in FAMS:
                lo = (grpctr % 2) * 4
                grpctr += 1
                bts = [bank_acq(lo + m4) for m4 in range(4)]
                tv = None
                for kt in range(G):
                    lsem, wv, wt, wi, kk = ev_wtile[(fam, c, mq, kt)]
                    P.wait("tensor", lsem, wv)
                    for m4 in range(4):
                        tv = P.op("tensor", lambda e, bk=bts[m4],
                                  w_=wt[:, kk * CH + m4 * 128:kk * CH + (m4 + 1) * 128],
                                  r_=xnT[:, kt * CH:(kt + 1) * CH], k_=kt:
                                  e.matmul(out=bk[:, 0:512], lhsT=w_, rhs=r_,
                                           start=(k_ == 0), stop=(k_ == G - 1)),
                                  "sT" if m4 == 3 else None)
                    if kk == 1:
                        wring[fam].set_free(wi, [("sT", tv)])
                        emit_wload(fam, (mq * G + kt + 1) // 2 + 1)
                ev_D[(c, mq, fam)] = tv
                P.wait("scalar", "sT", tv)
                ring = {"in": rI, "dec": rD, "gate": rG}[fam]
                for m4 in range(4):
                    g = mq * 4 + m4
                    et = ring.acquire(P, "scalar", c * G + g)
                    evv = P.op("scalar", lambda e, bk=bts[m4], g_=g, o_=et[:], f_=fam:
                               e.activation(out=o_, in_=bk[:, 0:512], func=FAM_AF[f_],
                                            bias=pcol(FAM_BIAS[f_], g_)), "sS")
                    ev_evac[(c, fam, g)] = (evv, et)
                    bank_rel(lo + m4, "sS", evv)
                    if dbg:
                        dt_map = {"in": "dI", "dec": "dD", "gate": "dG"}
                        dbg_idg_conds[(c, fam, g)] = dbg_dump(
                            "sS", evv, et[:],
                            dbg_t[dt_map[fam]][g, :, c * CH:(c + 1) * CH])
            qg = []
            for m4 in range(4):
                g = mq * 4 + m4
                gidx = c * G + g
                iv, it_ = ev_evac[(c, "in", g)]
                dv, dt_ = ev_evac[(c, "dec", g)]
                gv, gt_ = ev_evac[(c, "gate", g)]
                ut = rU.acquire(P, "vector", gidx)
                P.wait("vector", "sS", max(iv, dv))
                uv = P.op("vector", lambda e, d_=dt_[:], i_=it_[:], o_=ut[:]:
                          e.scalar_tensor_tensor(out=o_, in0=d_, scalar=1.0, in1=i_,
                                                 op0=ALU.subtract, op1=ALU.mult), "sV")
                ev_u[(c, g)] = uv
                rI.set_free(gidx, [("sV", uv)] + ([dbg_idg_conds[(c, "in", g)]] if dbg else []))
                qg.append((g, gidx, ut, it_, dt_, gt_))
            scans = []
            for g, gidx, ut, it_, dt_, gt_ in qg:
                lt = rL.acquire(P, "vector", gidx)
                pt = rP.acquire(P, "vector", gidx)
                if c > 0:
                    P.wait("vector", "sS", ev_carry[(c - 1, g)])
                P.op("vector", lambda e, d_=dt_[:], u_=ut[:], o_=lt[:],
                     i_=bnd[:, g:g + 1]: e.tensor_tensor_scan(
                         out=o_, data0=d_, data1=u_, initial=i_,
                         op0=ALU.mult, op1=ALU.add))
                sv = P.op("vector", lambda e, d_=dt_[:], o_=pt[:],
                          i_=bnd[:, 16 + g:17 + g]: e.tensor_tensor_scan(
                              out=o_, data0=d_, data1=zeros[:], initial=i_,
                              op0=ALU.mult, op1=ALU.add), "sV")
                ev_scan[(c, g)] = sv
                rU.set_free(gidx, [("sV", sv)])
                scans.append((g, gidx, lt, pt, dt_, gt_, sv))
            for g, gidx, lt, pt, dt_, gt_, sv in scans:
                P.wait("scalar", "sV", sv)
                P.op("scalar", lambda e, l_=lt[:, CH - 1:CH], o_=bnd[:, g:g + 1]:
                     e.activation(out=o_, in_=l_, func=AF.Identity))
                cv = P.op("scalar", lambda e, p_=pt[:, CH - 1:CH],
                          o_=bnd[:, 16 + g:17 + g]: e.activation(
                              out=o_, in_=p_, func=AF.Identity), "sS")
                ev_carry[(c, g)] = cv
                dbg_lp = []
                if dbg:
                    dbg_lp = [dbg_dump("sV", sv, lt[:], dbg_t["dL"][g, :, c * CH:(c + 1) * CH]),
                              dbg_dump("sV", sv, pt[:], dbg_t["dP"][g, :, c * CH:(c + 1) * CH])]
                rD.set_free(gidx, [("sV", sv)] + ([dbg_idg_conds[(c, "dec", g)]] if dbg else []))
                at_ = rAs.acquire(P, "vector", gidx)
                bt_ = rBs.acquire(P, "vector", gidx)
                P.wait("vector", "sS", ev_evac[(c, "gate", g)][0])
                va_ = P.op("vector", lambda e, o_=at_[:], l_=lt[:], g2_=gt_[:]:
                           e.tensor_tensor(out=o_, in0=l_, in1=g2_, op=ALU.mult), "sV")
                vb_ = P.op("vector", lambda e, o_=bt_[:], p_=pt[:], g2_=gt_[:]:
                           e.tensor_tensor(out=o_, in0=p_, in1=g2_, op=ALU.mult), "sV")
                sa = f"dSpA{gidx % 4}"
                sb = f"dSpB{gidx % 4}"
                P.wait("gpsimd", "sV", vb_)
                spa = P.op("gpsimd", lambda e, s_=at_[:],
                           d_=sca.ap()[g, :, c * CH:(c + 1) * CH]:
                           e.dma_start(out=d_, in_=s_), sa, 16)
                spb = P.op("gpsimd", lambda e, s_=bt_[:],
                           d_=scb.ap()[g, :, c * CH:(c + 1) * CH]:
                           e.dma_start(out=d_, in_=s_), sb, 16)
                ev_spill[("a", c, g)] = (sa, spa)
                ev_spill[("b", c, g)] = (sb, spb)
                rAs.set_free(gidx, [(sa, spa)])
                rBs.set_free(gidx, [(sb, spb)])
                rG.set_free(gidx, [("sV", vb_)] + ([dbg_idg_conds[(c, "gate", g)]] if dbg else []))
                rL.set_free(gidx, [("sV", va_), ("sS", cv)] + dbg_lp[:1])
                rP.set_free(gidx, [("sV", vb_), ("sS", cv)] + dbg_lp[1:])

    P.wait("gpsimd", "sS", ev_carry[(nchunk - 1, G - 1)])
    agv = P.op("gpsimd", lambda e: e.dma_start(out=ag_in.ap()[:, :], in_=bnd[:]), "dAgIn", 16)
    P.wait("gpsimd", "dAgIn", agv)
    P.op("gpsimd", lambda e: e.collective_compute(
        "AllGather", ALU.bypass, replica_groups=[list(range(n_ranks))],
        ins=[ag_in.ap().opt()], outs=[ag_out.ap().opt()]).then_inc(P.hs["sCC"]))
    P.ops["gpsimd"].append(lambda e: e.wait_ge(P.hs["sCC"], 1))
    gl = None
    for j in range(n_ranks):
        gl = P.op("gpsimd", lambda e, o_=gath[:, j * 32:(j + 1) * 32],
                  s_=ag_out.ap()[j * 128:(j + 1) * 128, :]:
                  e.dma_start(out=o_, in_=s_), "dAgOut", 16)
    P.wait("vector", "dAgOut", gl)
    cur = 0
    for j in range(n_ranks):
        Lj = gath[:, j * 32:j * 32 + 16]
        Pj = gath[:, j * 32 + 16:j * 32 + 32]
        P.op("vector", lambda e, P_=Pj, s_=sinP[cur][:]: e.tensor_tensor(
            out=sinT[0][:], in0=P_, in1=s_, op=ALU.mult))
        P.op("vector", lambda e, L_=Lj: e.tensor_tensor(
            out=sinT[0][:], in0=sinT[0][:], in1=L_, op=ALU.add))
        P.op("vector", lambda e, s_=sinP[cur][:]: e.tensor_tensor(
            out=sinT[1][:], in0=sinT[0][:], in1=s_, op=ALU.subtract))
        P.op("vector", lambda e, m_=selm[:, j:j + 1], s_=sinP[cur][:],
             o_=sinP[1 - cur][:]: e.scalar_tensor_tensor(
                 out=o_, in0=sinT[1][:], scalar=m_, in1=s_,
                 op0=ALU.mult, op1=ALU.add))
        cur = 1 - cur
    sv_comb = P.op("vector", lambda e: e.nop(), "sV")
    P.wait("scalar", "sV", sv_comb)
    ev_sinF = P.op("scalar", lambda e, i_=sinP[cur][:]: e.activation(
        out=sinF[:], in_=i_, func=AF.Identity), "sS")

    ev_y = {}; ev_D2 = {}; ev_z = {}; ev_zT = {}; ev_fin = {}
    ev_w2 = {}; ev_x2 = {}; ev_abld = {}; ev_sw2 = {}; ev_atld = {}; ev_C2 = {}
    rGl2 = Ring(rL.tiles)
    rA2 = Ring(rA2.tiles + rD.tiles[0:2] + rP.tiles[0:2])
    rB2 = Ring(rB2.tiles + rD.tiles[2:4] + rP.tiles[2:4])
    last_D_all = ev_D[(nchunk - 1, 3, "gate")]
    def emit_x2load(i):
        if i < 0 or i >= NXT:
            return
        x2t = rX2.acquire(P, "gpsimd", NXT + i)
        lsem = f"dX{i % 2}"
        lv = P.op("gpsimd", lambda e, d_=x2t[:], s_=x_d[i * 128:(i + 1) * 128, :]:
                  e.dma_start(out=d_, in_=s_), lsem, 16)
        ev_x2[i] = (lsem, lv, x2t)

    for s2 in ([f"dSpA{i}" for i in range(4)] + [f"dSpB{i}" for i in range(4)] +
               ["dSpT0", "dSpT1"]):
        P.wait("sync", s2, P.cnt[s2])
        P.wait("scalar", s2, P.cnt[s2])
        P.wait("gpsimd", s2, P.cnt[s2])

    emit_x2load(0)
    emit_x2load(1)

    def emit_abload_gi(gi):
        if gi >= nchunk * G:
            return
        c2, g = gi // G, gi % G
        at2 = rA2.acquire(P, "scalar", gi)
        bt2 = rB2.acquire(P, "gpsimd", gi)
        sa2 = f"dPA{gi % 6}"
        sb2 = f"dPB{gi % 6}"
        va = P.op("scalar", lambda e, d_=at2[:], s_=sca.ap()[g, :, c2 * CH:(c2 + 1) * CH]:
                  e.dma_start(out=d_, in_=s_), sa2, 16)
        vb = P.op("gpsimd", lambda e, d_=bt2[:], s_=scb.ap()[g, :, c2 * CH:(c2 + 1) * CH]:
                  e.dma_start(out=d_, in_=s_), sb2, 16)
        ev_abld[(c2, g)] = ((sa2, va), (sb2, vb), at2, bt2)

    for gi0 in range(6):
        emit_abload_gi(gi0)

    for c in range(nchunk):

        w2_tiles = None

        def emit_w2load(j, c=c):
            if j >= 32:
                return
            mq_, kp = j // 8, j % 8
            if c == 0 and j == 0:
                P.wait("sync", "dCv_out", ev_conv["out"])
            wt = wring["out"].acquire(P, "sync", c * 32 + j)
            lsem = f"dW_out{(c * 32 + j) % 2}"
            wv = None
            for hp in range(2):
                src = wsc["out"].ap()[mq_, kp, hp * 64:(hp + 1) * 64, :]
                wv = P.op("sync", lambda e, d_=wt[hp * 64:(hp + 1) * 64, :], s_=src:
                          e.dma_start(out=d_, in_=s_), lsem, 16)
            for kk in range(2):
                ev_w2[(c, mq_, kp * 2 + kk)] = (lsem, wv, wt, c * 32 + j, kk)

        def emit_swload2(j, c=c):
            if j >= 32:
                return
            oq_, kp = j // 8, j % 8
            if c == 0 and j == 0:
                P.wait("sync", "dCv_sw", ev_conv["sw"])
            wt = swring.acquire(P, "sync", c * 32 + j)
            lsem = f"dWsw{(c * 32 + j) % 2}"
            wv = None
            for hp in range(2):
                src = swsc.ap()[oq_, kp, hp * 64:(hp + 1) * 64, :]
                wv = P.op("sync", lambda e, d_=wt[hp * 64:(hp + 1) * 64, :], s_=src:
                          e.dma_start(out=d_, in_=s_), lsem, 16)
            for kk in range(2):
                ev_sw2[(c, oq_, kp * 2 + kk)] = (lsem, wv, wt, c * 32 + j, kk)

        def emit_atload(c=c):
            if c > 0:
                P.wait("sync", "sT", ev_C2[(c - 1, 3)])
            vals = []
            for h in range(2):
                lsem = f"dPT{h}"
                lv = None
                for hp in range(2):
                    src2 = scat.ap()[h * 8:(h + 1) * 8, hp * 64:(hp + 1) * 64,
                                     c * CH:(c + 1) * CH]
                    lv = P.op("sync", lambda e, d_=aT[hp * 64:(hp + 1) * 64,
                                                      h * 8 * CH:(h + 1) * 8 * CH].rearrange(
                        "p (g s) -> p g s", g=8), s_=src2.rearrange("g p s -> p g s"):
                        e.dma_start(out=d_, in_=s_), lsem, 16)
                vals.append((lsem, lv))
            return vals

        ev_atld[c] = emit_atload()
        for j in range(2):
            emit_swload2(j)
            emit_w2load(j)
        P.wait("vector", "sS", ev_sinF)
        if c == 0:
            P.wait("vector", "sT", last_D_all)
            P.wait("vector", "dCv_out", ev_conv["out"])
        for g in range(G):
            (sa2, va), (sb2, vb), at2, bt2 = ev_abld[(c, g)]
            P.wait("vector", sa2, va)
            P.wait("vector", sb2, vb)
            if c > 0:
                P.wait("vector", "sT", ev_D2[(c - 1, 3)])
            yv = P.op("vector", lambda e, a_=at2[:], b_=bt2[:], m_=sinF[:, g:g + 1],
                      o_=xnT[:, g * CH:(g + 1) * CH]:
                      e.scalar_tensor_tensor(out=o_, in0=b_, scalar=m_, in1=a_,
                                             op0=ALU.mult, op1=ALU.add), "sV")
            ev_y[(c, g)] = yv
            rA2.set_free(c * G + g, [("sV", yv)])
            rB2.set_free(c * G + g, [("sV", yv)])
            emit_abload_gi(c * G + g + 6)

        for h2, (ls2, lv2) in enumerate(ev_atld[c]):
            P.wait("tensor", ls2, lv2)
        grp2 = 0
        for q in range(4):
            lo = (grp2 % 2) * 4
            grp2 += 1
            bts = [bank_acq(lo + m4) for m4 in range(4)]
            tv = None
            for kt in range(G):
                lsem, wv, wt, wi, kk = ev_sw2[(c, q, kt)]
                P.wait("tensor", lsem, wv)
                for m4 in range(4):
                    tv = P.op("tensor", lambda e, bk=bts[m4],
                              w_=wt[:, kk * CH + m4 * 128:kk * CH + (m4 + 1) * 128],
                              r_=aT[:, kt * CH:(kt + 1) * CH], k_=kt:
                              e.matmul(out=bk[:, 0:512], lhsT=w_, rhs=r_,
                                       start=(k_ == 0), stop=(k_ == G - 1)),
                              "sT" if m4 == 3 else None)
                if kk == 1:
                    swring.set_free(wi, [("sT", tv)])
                    emit_swload2((q * G + kt + 1) // 2 + 1)
            ev_C2[(c, q)] = tv
            P.wait("scalar", "sT", tv)
            gq = []
            for m4 in range(4):
                o = q * 4 + m4
                gt = rGl2.acquire(P, "scalar", c * G + o)
                gv = P.op("scalar", lambda e, bk=bts[m4], o_=o, g_=gt[:]: e.activation(
                    out=g_, in_=bk[:, 0:512], func=AF.Silu,
                    scale=pcol("alpha", o_), bias=pcol("alphabias", o_)), "sS")
                gq.append((gv, gt))
                bank_rel(lo + m4, "sS", gv)
            lo = (grp2 % 2) * 4
            grp2 += 1
            bts = [bank_acq(lo + m4) for m4 in range(4)]
            tv = None
            for kt in range(G):
                lsem, wv, wt, wi, kk = ev_w2[(c, q, kt)]
                P.wait("tensor", lsem, wv)
                P.wait("tensor", "sV", ev_y[(c, kt)])
                for m4 in range(4):
                    tv = P.op("tensor", lambda e, bk=bts[m4],
                              w_=wt[:, kk * CH + m4 * 128:kk * CH + (m4 + 1) * 128],
                              r_=xnT[:, kt * CH:(kt + 1) * CH], k_=kt:
                              e.matmul(out=bk[:, 0:512], lhsT=w_, rhs=r_,
                                       start=(k_ == 0), stop=(k_ == G - 1)),
                              "sT" if m4 == 3 else None)
                if kk == 1:
                    wring["out"].set_free(wi, [("sT", tv)])
                    emit_w2load((q * G + kt + 1) // 2 + 1)
            ev_D2[(c, q)] = tv
            P.wait("scalar", "sT", tv)
            for m4 in range(4):
                m = q * 4 + m4
                st_ = rSs.acquire(P, "scalar", c * G + m)
                sv2 = P.op("scalar", lambda e, bk=bts[m4], m_=m, o_=st_[:]:
                           e.activation(out=o_, in_=bk[:, 0:512], func=AF.Identity,
                                        scale=-1.0, bias=pcol("b_out", m_)), "sS")
                bank_rel(lo + m4, "sS", sv2)
                gv, gt2 = gq[m4]
                P.wait("vector", "sS", max(sv2, gv))
                if c > 0:
                    P.wait("vector", "sT", ev_zT[(c - 1, TPC - 1, 3)])
                zv = P.op("vector", lambda e, s_=st_[:], g_=gt2[:],
                          o_=zt[:, m * CH:(m + 1) * CH]: e.tensor_tensor(
                              out=o_, in0=s_, in1=g_, op=ALU.mult), "sV")
                ev_z[(c, m)] = zv
                rSs.set_free(c * G + m, [("sV", zv)])
                rGl2.set_free(c * G + m, [("sV", zv)])
        for t in range(TPC):
            ot = rOut.acquire(P, "vector", c * TPC + t)
            x2sem, lxv, x2t = ev_x2[c * TPC + t]
            for j in range(4):
                P.wait("tensor", "sV", ev_z[(c, 4 * j + 3)])
                b = (t % 2) * 4 + j
                bt = bank_acq(b)
                tv = None
                for qq in range(4):
                    g = 4 * j + qq
                    tv = P.op("tensor", lambda e, bk=bt, q_=qq,
                              s_=zt[:, g * CH + t * 128:g * CH + (t + 1) * 128]:
                              e.transpose(out=bk[:, q_ * 128:(q_ + 1) * 128],
                                          in_=s_, identity=ident[:]),
                              "sT" if qq == 3 else None)
                ev_zT[(c, t, j)] = tv
                P.wait("vector", "sT", tv)
                P.wait("vector", x2sem, lxv)
                fv = P.op("vector", lambda e, bk=bt, x_=x2t[:, j * 512:(j + 1) * 512],
                          o_=ot[:, j * 512:(j + 1) * 512]: e.tensor_tensor(
                              out=o_, in0=x_, in1=bk[:, 0:512], op=ALU.add), "sV")
                ev_fin[(c, t, j)] = fv
                bank_rel(b, "sV", fv)
            rX2.set_free(NXT + c * TPC + t, [("sV", ev_fin[(c, t, 3)])])
            emit_x2load(c * TPC + t + 2)
            P.wait("gpsimd", "sV", ev_fin[(c, t, 3)])
            idx = c * TPC + t
            osem = f"dOut{(c * TPC + t) % 2}"
            ov = P.op("gpsimd", lambda e, s_=ot[:], d_=out_d[idx * 128:(idx + 1) * 128, :]:
                      e.dma_start(out=d_, in_=s_), osem, 16)
            rOut.set_free(c * TPC + t, [(osem, ov)])

    P.wait("gpsimd", "dOut0", P.cnt["dOut0"])
    P.wait("gpsimd", "dOut1", P.cnt["dOut1"])

    with stack, nc.Block() as block:
        block.sync(lambda e: P.run("sync", e))
        block.scalar(lambda e: P.run("scalar", e))
        block.vector(lambda e: P.run("vector", e))
        block.tensor(lambda e: P.run("tensor", e))
        block.gpsimd(lambda e: P.run("gpsimd", e))
    nc._plan_trace = P.trace
    return nc


def _core_inputs(inputs, core, n_ranks=NCORE):
    b, q = core // 4, core % 4
    x = np.asarray(inputs["L0_attn_x"], np.float32)
    m = {
        "x": np.ascontiguousarray(x[b, q * 4096:(q + 1) * 4096, :]),
        "w_in": np.asarray(inputs["L0_attn_W_in"], np.float32),
        "w_dec": np.asarray(inputs["L0_attn_W_decay"], np.float32),
        "w_gate": np.asarray(inputs["L0_attn_W_gate"], np.float32),
        "w_out": np.asarray(inputs["L0_attn_W_out"], np.float32),
        "ffnw": np.asarray(inputs["L0_ffn_weight"], np.float32),
        "b_in": np.asarray(inputs["L0_attn_b_in"], np.float32),
        "b_dec": np.asarray(inputs["L0_attn_b_decay"], np.float32),
        "b_gate": np.asarray(inputs["L0_attn_b_gate"], np.float32),
        "b_out": np.asarray(inputs["L0_attn_b_out"], np.float32),
        "dbias": np.asarray(inputs["L0_attn_decay_bias"], np.float32),
        "gamma": np.asarray(inputs["L0_ffn_gamma"], np.float32),
        "beta": np.asarray(inputs["L0_ffn_beta"], np.float32),
        "ffnb": np.asarray(inputs["L0_ffn_bias"], np.float32),
        "alpha": np.asarray(inputs["L0_ffn_alpha"], np.float32),
        "ident": np.eye(128, dtype=np.float32),
    }
    sel = np.zeros(8, np.float32)
    for j in range(n_ranks):
        if j // 4 == b and j % 4 < q:
            sel[j] = 1.0
    m["selm"] = np.tile(sel[None, :], (128, 1)).astype(np.float32)
    return m


def kernel(**inputs):
    nc = build_program(nchunk=8, n_ranks=NCORE)
    in_maps = [_core_inputs(inputs, c) for c in range(NCORE)]
    res = run_bass_kernel_spmd(nc, in_maps, core_ids=list(range(NCORE)))
    out = np.zeros((2, 16384, 2048), np.float32)
    for c in range(NCORE):
        b, q = c // 4, c % 4
        out[b, q * 4096:(q + 1) * 4096, :] = res.results[c]["out"]
    return out


# version 14
# speedup vs baseline: 1.0333x; 1.0333x over previous
import numpy as np
import concourse.bass as bass
import concourse.mybir as mybir
from concourse.bass_utils import run_bass_kernel_spmd
from concourse.replica_groups import maybe_share_collective_output_space
from contextlib import ExitStack

F32 = mybir.dt.float32
BF16 = mybir.dt.bfloat16
FP8 = mybir.dt.float8e4
AF = mybir.ActivationFunctionType
ALU = mybir.AluOpType
AX = mybir.AxisListType

D = 2048
G = 16
CH = 512
TPC = 4
NCORE = 8
INV_D = 1.0 / D
EPS = 1e-5
ENGS = ("sync", "scalar", "vector", "tensor", "gpsimd")
SEM_OWNER = {"sV": "vector", "sS": "scalar", "sT": "tensor", "sG": "gpsimd"}
FAMS = ("in", "dec", "gate")
FAM_AF = {"in": AF.Silu, "dec": AF.Sigmoid, "gate": AF.Sigmoid}
FAM_BIAS = {"in": "b_in", "dec": "bdd", "gate": "b_gate"}


class Plan:
    def __init__(self, nc):
        self.nc = nc
        self.ops = {e: [] for e in ENGS}
        self.hs = {}
        self.cnt = {}
        self.last_wait = {}
        self.trace = []

    def new_sem(self, stack, name):
        self.hs[name] = stack.enter_context(self.nc.semaphore(name))
        self.cnt[name] = 0

    def wait(self, eng, name, val):
        if val is None or val <= 0:
            return
        own = SEM_OWNER.get(name)
        if own == eng:
            assert val <= self.cnt[name], f"own-engine future wait {eng} {name} {val}"
            return
        key = (eng, name)
        if self.last_wait.get(key, 0) >= val:
            return
        self.last_wait[key] = val
        self.trace.append((eng, "wait", name, val))
        h = self.hs[name]
        self.ops[eng].append(lambda e, h=h, v=val: e.wait_ge(h, v))

    def op(self, eng, build, inc=None, incn=1):
        import traceback
        ln = traceback.extract_stack()[-2].lineno
        self.trace.append((eng, "op", inc, self.cnt.get(inc, 0) + incn if inc else None, ln))
        if inc is not None:
            self.cnt[inc] += incn
            after = self.cnt[inc]
            h = self.hs[inc]
            self.ops[eng].append(lambda e, b=build, h=h, n=incn: b(e).then_inc(h, n))
            return after
        self.ops[eng].append(lambda e, b=build: b(e))
        return None

    def run(self, eng_name, eng):
        for fn in self.ops[eng_name]:
            fn(eng)


class Ring:
    def __init__(self, tiles):
        self.tiles = tiles
        self.n = len(tiles)
        self.free = [[] for _ in range(self.n)]

    def acquire(self, P, eng, idx):
        s = idx % self.n
        for sem, val in self.free[s]:
            P.wait(eng, sem, val)
        self.free[s] = []
        return self.tiles[s]

    def set_free(self, idx, conds):
        self.free[idx % self.n] = [c for c in conds if c[1]]


def build_program(nchunk=8, n_ranks=NCORE, dbg=False, phase1_only=False, skip_c=False, xnt_input=False):
    nc = bass.Bass("TRN2", target_bir_lowering=False)
    S = nchunk * CH

    x_d = nc.dram_tensor("x", [S, D], F32, kind="ExternalInput").ap()
    fam_src = {}
    for f, nm in (("in", "w_in"), ("dec", "w_dec"), ("gate", "w_gate"), ("out", "w_out")):
        fam_src[f] = nc.dram_tensor(nm, [D, D], F32, kind="ExternalInput").ap()
    ffnw_d = nc.dram_tensor("ffnw", [D, D], F32, kind="ExternalInput").ap()
    VNAMES = ["b_in", "b_dec", "b_gate", "b_out", "dbias", "gamma", "beta", "ffnb", "alpha"]
    vecs_d = {vn: nc.dram_tensor(vn, [D], F32, kind="ExternalInput").ap() for vn in VNAMES}
    ident_d = nc.dram_tensor("ident", [128, 128], F32, kind="ExternalInput").ap()
    xnt_d = (nc.dram_tensor("xnt_in", [nchunk, G * 128, CH], BF16, kind="ExternalInput").ap()
             if xnt_input else None)
    selm_d = nc.dram_tensor("selm", [128, 8], F32, kind="ExternalInput").ap()
    out_d = nc.dram_tensor("out", [S, D], F32, kind="ExternalOutput").ap()

    kd = "ExternalOutput" if dbg else "Internal"
    wsc = {f: nc.dram_tensor(f"wsc_{f}", [4, 8, 128, 1024], BF16, kind=kd) for f in ("in", "dec", "gate", "out")}
    swsc = nc.dram_tensor("swsc", [4, 8, 128, 1024], FP8, kind=kd)
    sca = nc.dram_tensor("sca", [G, 128, S], F32, kind=kd)
    scat = nc.dram_tensor("scat", [G, 128, S], FP8, kind=kd)
    scb = nc.dram_tensor("scb", [G, 128, S], F32, kind=kd)
    ag_space = maybe_share_collective_output_space(
        "AllGather", [list(range(n_ranks))], prefer_local=True)
    ag_in = nc.dram_tensor("ag_in", [128, 32], F32)
    if dbg:
        dbg_t = {
            "dXN": nc.dram_tensor("dXN", [S, D], F32, kind="ExternalOutput").ap(),
            "dQ": nc.dram_tensor("dQ", [S, D], F32, kind="ExternalOutput").ap(),
            "dXNT": nc.dram_tensor("dXNT", [G, 128, S], BF16, kind="ExternalOutput").ap(),
            "dAT": nc.dram_tensor("dAT", [G, 128, S], FP8, kind="ExternalOutput").ap(),
            "dI": nc.dram_tensor("dI", [G, 128, S], F32, kind="ExternalOutput").ap(),
            "dD": nc.dram_tensor("dD", [G, 128, S], F32, kind="ExternalOutput").ap(),
            "dG": nc.dram_tensor("dG", [G, 128, S], F32, kind="ExternalOutput").ap(),
            "dL": nc.dram_tensor("dL", [G, 128, S], F32, kind="ExternalOutput").ap(),
            "dP": nc.dram_tensor("dP", [G, 128, S], F32, kind="ExternalOutput").ap(),
            "dST": nc.dram_tensor("dST", [nchunk * TPC, 128, 48], F32, kind="ExternalOutput").ap(),
            "dPV": nc.dram_tensor("dPV", [128, 16 * 11], F32, kind="ExternalOutput").ap(),
            "dAS": nc.dram_tensor("dAS", [G, 128, S], F32, kind="ExternalOutput").ap(),
            "dSWF": nc.dram_tensor("dSWF", [G, 4, 128, 512], F32, kind="ExternalOutput").ap(),
            "dRM": nc.dram_tensor("dRM", [G, 128, 8], F32, kind="ExternalOutput").ap(),
        }
    ag_out = nc.dram_tensor("ag_out", [128 * n_ranks, 32], F32, addr_space=ag_space)

    stack = ExitStack()
    sb = lambda name, shape, dt: stack.enter_context(nc.sbuf_tensor(name, shape, dt))

    ident = sb("identsb", [128, 128], F32)
    SLOT = {n: i for i, n in enumerate(VNAMES + ["bdd", "alphabias"])}
    pvec = sb("pvec", [128, 16 * len(SLOT)], F32)
    selm = sb("selmsb", [128, 8], F32)
    epsc = sb("epsc", [128, 1], F32)
    zeros = sb("zerosc", [128, CH], F32)
    bnd = sb("bnd", [128, 32], F32)
    gath = sb("gath", [128, 32 * n_ranks], F32)
    sinP = [sb(f"sin{i}", [128, 16], F32) for i in range(2)]
    sinT = [sb(f"sint{i}", [128, 16], F32) for i in range(2)]
    sinF = sb("sinF", [128, 16], F32)
    statc = sb("statc", [128, 48 * TPC], F32)

    xr = Ring([sb(f"xr{i}", [128, D], F32) for i in range(2)])
    xnr = Ring([sb(f"xnr{i}", [128, D], F32) for i in range(4)])
    qr = Ring([sb(f"qr{i}", [128, D], F32) for i in range(2)])
    xnT = sb("xnT", [128, G * CH], BF16)
    aT = sb("aT", [128, G * CH], FP8)
    rI = Ring([sb(f"ii{i}", [128, CH], F32) for i in range(4)])
    rD = Ring([sb(f"dd{i}", [128, CH], F32) for i in range(4)])
    rG = Ring([sb(f"gg{i}", [128, CH], F32) for i in range(4)])
    rU = Ring([sb(f"uu{i}", [128, CH], F32) for i in range(4)])
    rL = Ring([sb(f"ll{i}", [128, CH], F32) for i in range(4)])
    rP = Ring([sb(f"pp{i}", [128, CH], F32) for i in range(4)])
    rAs = Ring([sb(f"ast{i}", [128, CH], F32) for i in range(2)])
    rBs = Ring([sb(f"bst{i}", [128, CH], F32) for i in range(2)])
    rGs = Ring([sb(f"gst{i}", [128, CH], F32) for i in range(2)])
    wring = {f: Ring([sb(f"w_{f}_{i}", [128, 2 * CH], BF16) for i in range(2)])
             for f in ("in", "dec", "gate", "out")}
    swring = Ring([sb(f"swr{i}", [128, 2 * CH], FP8) for i in range(2)])
    rA2, rB2, rGl, rX2, rOut, rSs = rAs, rBs, rGs, xr, xnr, rI

    prostack = ExitStack()
    psb = lambda name, shape, dt: prostack.enter_context(nc.sbuf_tensor(name, shape, dt))
    convfV = [psb(f"convfV{i}", [128, 512], F32) for i in range(4)]
    convbV = [psb(f"convbV{i}", [128, 512], BF16) for i in range(4)]
    convfS = [psb(f"convfS{i}", [128, 512], F32) for i in range(2)]
    convbS = [psb(f"convbS{i}", [128, 512], BF16) for i in range(2)]
    sw8 = [psb(f"sw8_{i}", [128, D], FP8) for i in range(2)]
    swf = [psb(f"swf{i}", [128, 512], F32) for i in range(2)]
    swin = [psb("swin0", [128, D], F32)]
    rowm = psb("rowm", [128, 8], F32)
    prostack.close()
    zt = sb("zt", [128, G * CH], F32)

    pb = [stack.enter_context(nc.psum_tensor(f"pb{i}", [128, 512], F32)) for i in range(8)]
    bank_free = [[] for _ in range(8)]

    P = Plan(nc)
    sem_names = ["dPro", "dAgIn", "dAgOut", "dOut0", "dOut1", "dDbg",
                 "sV", "sS", "sT", "sG", "sCC"]
    sem_names += [f"dX{i}" for i in range(3)]
    for f in ("in", "dec", "gate", "out"):
        sem_names += [f"dW_{f}{i}" for i in range(4)]
        sem_names.append(f"dCv_{f}")
    sem_names += [f"dWsw{i}" for i in range(4)] + ["dCv_sw"]
    sem_names += [f"dSpA{i}" for i in range(4)] + [f"dSpB{i}" for i in range(4)]
    sem_names += [f"dSpC{i}" for i in range(4)] + [f"dSpG{i}" for i in range(2)]
    sem_names += [f"dPA{i}" for i in range(6)] + [f"dPB{i}" for i in range(6)]
    sem_names += [f"dPC{i}" for i in range(6)] + [f"dPG{i}" for i in range(2)]
    sem_names += ["dPLV0", "dPLV1", "dPLV2", "dPLV3", "dPLS0", "dPLS1", "dSwin0", "dSwin1"]
    sem_names += ["dSpT0", "dSpT1", "dPT0", "dPT1"]
    for s_ in sem_names:
        P.new_sem(stack, s_)

    def bank_acq(i):
        for sem, val in bank_free[i]:
            P.wait("tensor", sem, val)
        bank_free[i] = []
        return pb[i]

    def bank_rel(i, sem, val):
        bank_free[i] = [(sem, val)]

    def dbg_dump(wait_sem, wait_val, src_ap, dst_ap):
        P.wait("gpsimd", wait_sem, wait_val)
        v = P.op("gpsimd", lambda e, d_=dst_ap, s_=src_ap: e.dma_start(
            out=d_, in_=s_), "dDbg", 16)
        return ("dDbg", v)

    def pcol(slot, g):
        c = SLOT[slot] * 16 + g
        return pvec[:, c:c + 1]

    stat_cols = {"meanv": 32, "varv": 33, "mean1": 34, "lnv1": 35, "rs1": 36,
                 "mu2v": 37, "var2v": 38, "mu2": 39, "lnv2": 40, "rs2": 41,
                 "tmp0": 42, "tmp1": 43, "tmp2": 44, "msq": 45}

    def scol(name, t):
        c = t * 48 + stat_cols[name]
        return statc[:, c:c + 1]

    def bn6(t):
        return statc[:, t * 48:t * 48 + 24]

    def mvc(t, i):
        c = t * 48 + 24 + i
        return statc[:, c:c + 1]

    def emit_bn_stats(xt, t, mean_name, var_name):
        assert stat_cols[var_name] == stat_cols[mean_name] + 1
        for b in range(4):
            P.op("vector", lambda e, x_=xt[:, b * 512:(b + 1) * 512],
                 o_=bn6(t)[:, b * 6:(b + 1) * 6]: e.bn_stats(out=o_, in_=x_))
        for k in range(6):
            P.op("vector", lambda e, o_=scol("tmp0", t): e.tensor_copy(
                out=o_, in_=epsc[:]))
        return P.op("vector", lambda e, i_=bn6(t),
                    o_=statc[:, t * 48 + stat_cols[mean_name]:t * 48 + stat_cols[mean_name] + 2]:
                    e.bn_aggr(out=o_, in_=i_), "sV")

    P.op("sync", lambda e: e.dma_start(out=ident[:], in_=ident_d[:, :]), "dPro", 16)
    P.op("sync", lambda e: e.dma_start(out=selm[:], in_=selm_d[:, :]), "dPro", 16)
    for vn in VNAMES:
        dst = pvec[:, SLOT[vn] * 16:SLOT[vn] * 16 + 16]
        src = vecs_d[vn].rearrange("(g p) -> p g", p=128)
        P.op("sync", lambda e, d_=dst, s_=src: e.dma_start(
            out=d_, in_=s_, allow_slow_non_contiguous=True), "dPro", 16)
    ev_pro_loads = P.cnt["dPro"]

    P.op("gpsimd", lambda e: e.memset(epsc[:], EPS))
    P.op("gpsimd", lambda e: e.memset(zeros[:], 0.0))
    P.op("gpsimd", lambda e: e.memset(bnd[:, 0:16], 0.0))
    P.op("gpsimd", lambda e: e.memset(bnd[:, 16:32], 1.0))
    ev_consts = P.op("gpsimd", lambda e: e.memset(sinP[0][:], 0.0), "sG")

    P.wait("vector", "dPro", ev_pro_loads)
    P.op("vector", lambda e: e.tensor_tensor(
        out=pvec[:, SLOT["bdd"] * 16:SLOT["bdd"] * 16 + 16],
        in0=pvec[:, SLOT["b_dec"] * 16:SLOT["b_dec"] * 16 + 16],
        in1=pvec[:, SLOT["dbias"] * 16:SLOT["dbias"] * 16 + 16], op=ALU.add))
    ev_pvec = P.op("vector", lambda e: e.tensor_tensor(
        out=pvec[:, SLOT["alphabias"] * 16:SLOT["alphabias"] * 16 + 16],
        in0=pvec[:, SLOT["alpha"] * 16:SLOT["alpha"] * 16 + 16],
        in1=pvec[:, SLOT["ffnb"] * 16:SLOT["ffnb"] * 16 + 16], op=ALU.mult), "sV")

    conv_ring = {"vector": Ring(list(zip(convfV, convbV))),
                 "scalar": Ring(list(zip(convfS, convbS)))}
    conv_nslot = {"vector": 4, "scalar": 2}
    conv_idx = {"vector": 0, "scalar": 0}
    ev_conv = {}

    def convert_tile(fam, eng, kt, h):
        esem = "sV" if eng == "vector" else "sS"
        i = conv_idx[eng]
        conv_idx[eng] += 1
        cf, cb = conv_ring[eng].acquire(P, "sync", i)
        lsem = ("dPLV" + str(i % 2)) if eng == "vector" else ("dPLS" + str(i % 2))
        src = fam_src[fam][kt * 128:(kt + 1) * 128, h * 512:(h + 1) * 512]
        ldv = P.op("sync", lambda e, d_=cf[:], s_=src: e.dma_start(
            out=d_, in_=s_), lsem, 16)
        P.wait(eng, lsem, ldv)
        if eng == "vector":
            cv = P.op(eng, lambda e, o=cb[:], i_=cf[:]: e.tensor_copy(
                out=o, in_=i_), esem)
        else:
            cv = P.op(eng, lambda e, o=cb[:], i_=cf[:]: e.activation(
                out=o, in_=i_, func=AF.Identity), esem)
        P.wait("gpsimd", esem, cv)
        dst = wsc[fam].ap()[h, kt // 2, :, (kt % 2) * 512:(kt % 2) * 512 + 512]
        stv = P.op("gpsimd", lambda e, d_=dst, s_=cb[:]: e.dma_start(
            out=d_, in_=s_), "dCv_" + fam, 16)
        conv_ring[eng].set_free(i, [(esem, cv), ("dCv_" + fam, stv)])

    def convert_fam(fam, eng, kts=None):
        for kt in (range(G) if kts is None else kts):
            for h in range(4):
                convert_tile(fam, eng, kt, h)
        ev_conv[fam] = P.cnt["dCv_" + fam]

    _swg = {"swf_free": [[], []], "sw8_free": [[], []], "swin_free": [[], []]}

    def emit_swgen_ot(ot):
        swf_free = _swg["swf_free"]
        sw8_free = _swg["sw8_free"]
        swin_free = _swg["swin_free"]
        if True:
            slot = ot % 2
            sl2 = 0
            for sem, val in swin_free[sl2]:
                P.wait("sync", sem, val)
            swin_free[sl2] = []
            ldv = P.op("sync", lambda e, d_=swin[sl2][:], s_=ffnw_d[ot * 128:(ot + 1) * 128, :]:
                       e.dma_start(out=d_, in_=s_), "dSwin" + str(sl2), 16)
            P.wait("vector", "dSwin" + str(sl2), ldv)
            rsv = P.op("vector", lambda e, s_=swin[sl2][:]: e.reduce_sum(
                out=rowm[:, 4 * (ot % 2):4 * (ot % 2) + 1], in_=s_, axis=AX.X), "sV")
            P.wait("scalar", "sV", rsv)
            nmS = P.op("scalar", lambda e: e.activation(out=rowm[:, 4 * (ot % 2) + 1:4 * (ot % 2) + 2], in_=rowm[:, 4 * (ot % 2):4 * (ot % 2) + 1],
                                                        func=AF.Identity, scale=-INV_D), "sS")
            P.wait("vector", "sS", nmS)
            nmV = P.op("vector", lambda e: e.tensor_copy(out=rowm[:, 4 * (ot % 2) + 2:4 * (ot % 2) + 3], in_=rowm[:, 4 * (ot % 2) + 1:4 * (ot % 2) + 2]), "sV")
            P.wait("scalar", "sV", nmV)
            for sem, val in sw8_free[slot]:
                P.wait("vector", sem, val)
            sw8_free[slot] = []
            sgs = []
            for h in range(4):
                for sem, val in swf_free[h % 2]:
                    P.wait("scalar", sem, val)
                swf_free[h % 2] = []
                sg = P.op("scalar", lambda e, o_=swf[h % 2][:],
                          i_=swin[sl2][:, h * 512:(h + 1) * 512]: e.activation(
                              out=o_, in_=i_, func=AF.Sign, bias=rowm[:, 4 * (ot % 2) + 2:4 * (ot % 2) + 3]), "sS")
                sgs.append(sg)
                if dbg:
                    dbg_dump("sS", sg, swf[h % 2][:], dbg_t["dSWF"][ot, h, :, :])
                    dbg_dump("sS", sg, rowm[:, 0:8], dbg_t["dRM"][ot, :, :])
                P.wait("tensor", "sS", sg)
                tvs = []
                for it in range(4):
                    b = (h * 4 + it) % 8
                    bt = bank_acq(b)
                    tv = P.op("tensor", lambda e, bk=bt, s_=swf[h % 2][:, it * 128:(it + 1) * 128]:
                              e.transpose(out=bk[:, 0:128], in_=s_, identity=ident[:]), "sT")
                    P.wait("vector", "sT", tv)
                    itg = h * 4 + it
                    jcol = (itg % 2) * 8 + itg // 2
                    ev = P.op("vector", lambda e, bk=bt, o_=sw8[slot][:, jcol * 128:(jcol + 1) * 128]:
                              e.tensor_copy(out=o_, in_=bk[:, 0:128]), "sV")
                    bank_rel(b, "sV", ev)
                    tvs.append(tv)
                swf_free[h % 2] = [("sT", tvs[-1])]
                if dbg:
                    swf_free[h % 2].append(("dDbg", P.cnt["dDbg"]))
            last_ev = P.cnt["sV"]
            swin_free[sl2] = [("sS", sgs[-1]), ("sV", rsv)]
            P.wait("gpsimd", "sV", last_ev)
            stv = None
            oq_, oc_ = ot // 4, (ot % 4) * 128
            for half in range(2):
                for h2 in range(2):
                    dst = swsc.ap()[oq_, half * 4:(half + 1) * 4, :,
                                    h2 * 512 + oc_:h2 * 512 + oc_ + 128]
                    dst = dst.rearrange("it p o -> p it o")
                    lo_ = (h2 * 8 + half * 4) * 128
                    src = sw8[slot][:, lo_:lo_ + 512].rearrange(
                        "p (it o) -> p it o", it=4)
                    stv = P.op("gpsimd", lambda e, d_=dst, s_=src: e.dma_start(
                        out=d_, in_=s_), "dCv_sw", 16)
            sw8_free[slot] = [("dCv_sw", stv)]
            if ot == G - 1:
                ev_conv["sw"] = P.cnt["dCv_sw"]


    ev_xload = {}; ev_xn = {}; ev_q = {}; ev_r2 = {}; ev_mu2 = {}
    ev_A = {}; ev_Aev = {}; ev_B = {}; ev_sign = {}
    ev_C = {}; ev_gffn = {}; ev_D = {}; ev_evac = {}
    ev_u = {}; ev_scan = {}; ev_ab = {}; ev_carry = {}
    ev_spill = {}; ev_wtile = {}

    P.wait("scalar", "sG", ev_consts)
    P.wait("vector", "sG", ev_consts)
    P.wait("scalar", "dPro", ev_pro_loads)
    P.wait("scalar", "sV", ev_pvec)
    if dbg:
        dbg_dump("sV", ev_pvec, pvec[:, 0:16 * 11], dbg_t["dPV"][:, :])
    dbg_xn_conds = {}; dbg_q_conds = {}; dbg_idg_conds = {}

    NXT = nchunk * TPC

    def emit_xload(i):
        if i < 0 or i >= NXT:
            return
        xt = xr.acquire(P, "sync", i)
        lsem = f"dX{i % 2}"
        lv = P.op("sync", lambda e, d_=xt[:], s_=x_d[i * 128:(i + 1) * 128, :]:
                  e.dma_start(out=d_, in_=s_), lsem, 16)
        ev_xload[i] = (lsem, lv, xt)

    emit_xload(0)
    emit_xload(1)

    def emit_stats(c, only_t=None):
        for t in (range(TPC) if only_t is None else [only_t]):
            idx = c * TPC + t
            lsem, lv, xt = ev_xload[idx]
            P.wait("vector", lsem, lv)
            if c > 0:
                P.wait("vector", "sS", ev_r2[(c - 1, t)])
            v1 = emit_bn_stats(xt, t, "meanv", "varv")
            P.wait("scalar", "sV", v1)
            m1 = P.op("scalar", lambda e, i_=scol("meanv", t), o_=scol("mean1", t):
                      e.activation(out=o_, in_=i_, func=AF.Identity), "sS")
            P.op("scalar", lambda e, i_=scol("varv", t), o_=scol("lnv1", t):
                 e.activation(out=o_, in_=i_, func=AF.Ln, bias=epsc[:]))
            r1 = P.op("scalar", lambda e, i_=scol("lnv1", t), o_=scol("rs1", t):
                      e.activation(out=o_, in_=i_, func=AF.Exp, scale=-0.5), "sS")
            xnt = xnr.acquire(P, "vector", idx)
            P.wait("vector", "sS", r1)
            xnv = P.op("vector", lambda e, x_=xt[:], o_=xnt[:], m_=scol("mean1", t),
                       r_=scol("rs1", t): e.tensor_scalar(
                           out=o_, in0=x_, scalar1=m_, scalar2=r_,
                           op0=ALU.subtract, op1=ALU.mult), "sV")
            ev_xn[(c, t)] = (xnv, xnt)
            xr.set_free(idx, [("sV", xnv)])
            emit_xload(idx + 2)
            v2 = emit_bn_stats(xnt, t, "mu2v", "var2v")
            P.wait("scalar", "sV", v2)
            m2 = P.op("scalar", lambda e, i_=scol("mu2v", t), o_=scol("mu2", t):
                      e.activation(out=o_, in_=i_, func=AF.Identity), "sS")
            P.op("scalar", lambda e, i_=scol("var2v", t), o_=scol("lnv2", t):
                 e.activation(out=o_, in_=i_, func=AF.Ln, bias=epsc[:]))
            r2 = P.op("scalar", lambda e, i_=scol("lnv2", t), o_=scol("rs2", t):
                      e.activation(out=o_, in_=i_, func=AF.Exp, scale=-0.5), "sS")
            ev_r2[(c, t)] = r2
            ev_mu2[(c, t)] = max(m2, r2)
            if dbg:
                dbg_dump("sS", r2, statc[:, t * 48:(t + 1) * 48],
                         dbg_t["dST"][c * TPC + t, :, :])
                dbg_xn_conds[(c, t)] = dbg_dump(
                    "sV", xnv, xnt[:], dbg_t["dXN"][idx * 128:(idx + 1) * 128, :])


    emit_stats(0)

    for _kt in range(G):
        for _h in range(4):
            convert_tile("in", "vector", _kt, _h)
            convert_tile("dec", "scalar", _kt, _h)
            convert_tile("gate", "vector", _kt, _h)
    for _f in ("in", "dec", "gate"):
        ev_conv[_f] = P.cnt["dCv_" + _f]

    for c in range(nchunk):
        if xnt_input:
            if c > 0:
                P.wait("sync", "sT", ev_D[(c - 1, 3, "gate")])
            lv = P.op("sync", lambda e, d_=xnT[:].rearrange("p (k s) -> p k s", k=G),
                      s_=xnt_d[c].rearrange("(k p) s -> p k s", p=128):
                      e.dma_start(out=d_, in_=s_), "dAgIn", 16)
            fake = P.op("vector", lambda e, v=lv: e.nop(), "sV")
            P.wait("vector", "dAgIn", lv)
            for t in range(TPC):
                ev_Aev[(c, t, 3)] = fake
                ev_A[(c, t, 3)] = P.cnt["sT"]
                ev_B[(c, t, 3)] = P.cnt["sT"]
                xnv, xnt = ev_xn[(c, t)]
                xnr.set_free(c * TPC + t, [("sV", fake)] +
                             ([dbg_xn_conds[(c, t)]] if dbg else []))
        if (not xnt_input) and c > 0:
            for sem2, val2 in ev_at_free_prev:
                P.wait("scalar", sem2, val2)
        for t in range(TPC) if not xnt_input else []:
            xnv, xnt = ev_xn[(c, t)]
            qt = qr.acquire(P, "vector", c * TPC + t)
            P.wait("vector", "sS", ev_mu2[(c, t)])
            qv = P.op("vector", lambda e, x_=xnt[:], o_=qt[:], m_=scol("mu2", t),
                      r_=scol("rs2", t): e.tensor_scalar(
                          out=o_, in0=x_, scalar1=m_, scalar2=r_,
                          op0=ALU.subtract, op1=ALU.mult), "sV")
            ev_q[(c, t)] = (qv, qt)
            if dbg:
                idx = c * TPC + t
                dbg_q_conds[(c, t)] = dbg_dump(
                    "sV", qv, qt[:], dbg_t["dQ"][idx * 128:(idx + 1) * 128, :])
            if c + 1 < nchunk:
                emit_stats(c + 1, only_t=t)
            P.wait("tensor", "sV", xnv)
            loA = (t % 2) * 4
            for j in range(4):
                b = loA + j
                bt = bank_acq(b)
                tv = None
                for qq in range(4):
                    g = 4 * j + qq
                    tv = P.op("tensor", lambda e, bk=bt, s_=xnt[:, g * 128:(g + 1) * 128],
                              q_=qq: e.transpose(out=bk[:, q_ * 128:(q_ + 1) * 128],
                                                 in_=s_, identity=ident[:]),
                              "sT" if qq == 3 else None)
                ev_A[(c, t, j)] = tv
                P.wait("scalar", "sT", tv)
                if c > 0 and t == 0 and j == 0:
                    P.wait("scalar", "sT", ev_D[(c - 1, 3, "gate")])
                o_ap = xnT[:].rearrange("p (g s) -> p g s", g=G)[
                    :, 4 * j:4 * j + 4, t * 128:(t + 1) * 128]
                last = P.op("scalar", lambda e, bk=bt, o_=o_ap:
                            e.activation(out=o_, in_=bk[:, 0:512].rearrange(
                                "p (q s) -> p q s", q=4), func=AF.Identity), "sS")
                ev_Aev[(c, t, j)] = last
                bank_rel(b, "sS", last)
            P.wait("tensor", "sV", qv)
            loB = ((t + 1) % 2) * 4
            for j in range(4):
                b = loB + j
                bt = bank_acq(b)
                tv = None
                for qq in range(4):
                    g = 4 * j + qq
                    tv = P.op("tensor", lambda e, bk=bt, s_=qt[:, g * 128:(g + 1) * 128],
                              q_=qq: e.transpose(out=bk[:, q_ * 128:(q_ + 1) * 128],
                                                 in_=s_, identity=ident[:]),
                              "sT" if qq == 3 else None)
                ev_B[(c, t, j)] = tv
                P.wait("scalar", "sT", tv)
                last = None
                for qq in range(4):
                    g = 4 * j + qq
                    last = P.op("scalar", lambda e, bk=bt, q_=qq, g_=g,
                                o_=aT[:, g * CH + t * 128:g * CH + (t + 1) * 128]:
                                e.activation(out=o_, in_=bk[:, q_ * 128:(q_ + 1) * 128],
                                             func=AF.Sign, scale=pcol("gamma", g_),
                                             bias=pcol("beta", g_)),
                                "sS" if qq == 3 else None)
                ev_sign[(c, t, j)] = last
                bank_rel(b, "sS", last)
            xn_conds = [("sT", ev_A[(c, t, 3)])]
            q_conds = [("sT", ev_B[(c, t, 3)])]
            if dbg:
                xn_conds.append(dbg_xn_conds[(c, t)])
                q_conds.append(dbg_q_conds[(c, t)])
            xnr.set_free(c * TPC + t, xn_conds)
            qr.set_free(c * TPC + t, q_conds)

        ev_at_free_prev = None
        ev_sign_all = P.cnt["sS"]
        for kt2 in range(2):
            st_sem = f"dSpT{(c * 2 + kt2) % 2}"
            P.wait("gpsimd", "sS", ev_sign[(c, TPC - 1, 3)])
            dst = scat.ap()[kt2 * 8:(kt2 + 1) * 8, :, c * CH:(c + 1) * CH]
            dst = dst.rearrange("g p s -> p g s")
            spv = P.op("gpsimd", lambda e, d_=dst,
                       s_=aT[:, kt2 * 8 * CH:(kt2 + 1) * 8 * CH].rearrange(
                           "p (g s) -> p g s", g=8): e.dma_start(out=d_, in_=s_),
                       st_sem, 16)
            ev_spill[("at", c, kt2)] = (st_sem, spv)
        ev_at_free_prev = [ev_spill[("at", c, 0)], ev_spill[("at", c, 1)]]

        for ot_ in range(c * 3, min(c * 3 + 3, G)):
            emit_swgen_ot(ot_)
        if 1 <= c <= 4:
            convert_fam("out", "scalar", kts=range((c - 1) * 4, c * 4))

        def emit_wload(fam, j, c=c):
            if j >= 32:
                return
            mq_, kp = j // 8, j % 8
            if c == 0 and j == 0:
                P.wait("sync", "dCv_" + fam, ev_conv[fam])
            wt = wring[fam].acquire(P, "sync", c * 32 + j)
            lsem = f"dW_{fam}{(c * 32 + j) % 2}"
            src = wsc[fam].ap()[mq_, kp, :, :]
            wv = P.op("sync", lambda e, d_=wt[:], s_=src: e.dma_start(
                out=d_, in_=s_), lsem, 16)
            for kk in range(2):
                ev_wtile[(fam, c, mq_, kp * 2 + kk)] = (lsem, wv, wt, c * 32 + j, kk)

        for fam in FAMS:
            for j in range(2):
                emit_wload(fam, j)
        P.wait("tensor", "sS", ev_Aev[(c, TPC - 1, 3)])
        grpctr = 0
        for mq in range(4):
            for fam in FAMS:
                lo = (grpctr % 2) * 4
                grpctr += 1
                bts = [bank_acq(lo + m4) for m4 in range(4)]
                tv = None
                for kt in range(G):
                    lsem, wv, wt, wi, kk = ev_wtile[(fam, c, mq, kt)]
                    P.wait("tensor", lsem, wv)
                    for m4 in range(4):
                        tv = P.op("tensor", lambda e, bk=bts[m4],
                                  w_=wt[:, kk * CH + m4 * 128:kk * CH + (m4 + 1) * 128],
                                  r_=xnT[:, kt * CH:(kt + 1) * CH], k_=kt:
                                  e.matmul(out=bk[:, 0:512], lhsT=w_, rhs=r_,
                                           start=(k_ == 0), stop=(k_ == G - 1)),
                                  "sT" if m4 == 3 else None)
                    if kk == 1:
                        wring[fam].set_free(wi, [("sT", tv)])
                        emit_wload(fam, (mq * G + kt + 1) // 2 + 1)
                ev_D[(c, mq, fam)] = tv
                P.wait("scalar", "sT", tv)
                ring = {"in": rI, "dec": rD, "gate": rG}[fam]
                for m4 in range(4):
                    g = mq * 4 + m4
                    et = ring.acquire(P, "scalar", c * G + g)
                    evv = P.op("scalar", lambda e, bk=bts[m4], g_=g, o_=et[:], f_=fam:
                               e.activation(out=o_, in_=bk[:, 0:512], func=FAM_AF[f_],
                                            bias=pcol(FAM_BIAS[f_], g_)), "sS")
                    ev_evac[(c, fam, g)] = (evv, et)
                    bank_rel(lo + m4, "sS", evv)
                    if dbg:
                        dt_map = {"in": "dI", "dec": "dD", "gate": "dG"}
                        dbg_idg_conds[(c, fam, g)] = dbg_dump(
                            "sS", evv, et[:],
                            dbg_t[dt_map[fam]][g, :, c * CH:(c + 1) * CH])
            qg = []
            for m4 in range(4):
                g = mq * 4 + m4
                gidx = c * G + g
                iv, it_ = ev_evac[(c, "in", g)]
                dv, dt_ = ev_evac[(c, "dec", g)]
                gv, gt_ = ev_evac[(c, "gate", g)]
                ut = rU.acquire(P, "vector", gidx)
                P.wait("vector", "sS", max(iv, dv))
                uv = P.op("vector", lambda e, d_=dt_[:], i_=it_[:], o_=ut[:]:
                          e.scalar_tensor_tensor(out=o_, in0=d_, scalar=1.0, in1=i_,
                                                 op0=ALU.subtract, op1=ALU.mult), "sV")
                ev_u[(c, g)] = uv
                rI.set_free(gidx, [("sV", uv)] + ([dbg_idg_conds[(c, "in", g)]] if dbg else []))
                qg.append((g, gidx, ut, it_, dt_, gt_))
            scans = []
            for g, gidx, ut, it_, dt_, gt_ in qg:
                lt = rL.acquire(P, "vector", gidx)
                pt = rP.acquire(P, "vector", gidx)
                if c > 0:
                    P.wait("vector", "sS", ev_carry[(c - 1, g)])
                P.op("vector", lambda e, d_=dt_[:], u_=ut[:], o_=lt[:],
                     i_=bnd[:, g:g + 1]: e.tensor_tensor_scan(
                         out=o_, data0=d_, data1=u_, initial=i_,
                         op0=ALU.mult, op1=ALU.add))
                sv = P.op("vector", lambda e, d_=dt_[:], o_=pt[:],
                          i_=bnd[:, 16 + g:17 + g]: e.tensor_tensor_scan(
                              out=o_, data0=d_, data1=zeros[:], initial=i_,
                              op0=ALU.mult, op1=ALU.add), "sV")
                ev_scan[(c, g)] = sv
                rU.set_free(gidx, [("sV", sv)])
                scans.append((g, gidx, lt, pt, dt_, gt_, sv))
            for g, gidx, lt, pt, dt_, gt_, sv in scans:
                P.wait("scalar", "sV", sv)
                P.op("scalar", lambda e, l_=lt[:, CH - 1:CH], o_=bnd[:, g:g + 1]:
                     e.activation(out=o_, in_=l_, func=AF.Identity))
                cv = P.op("scalar", lambda e, p_=pt[:, CH - 1:CH],
                          o_=bnd[:, 16 + g:17 + g]: e.activation(
                              out=o_, in_=p_, func=AF.Identity), "sS")
                ev_carry[(c, g)] = cv
                dbg_lp = []
                if dbg:
                    dbg_lp = [dbg_dump("sV", sv, lt[:], dbg_t["dL"][g, :, c * CH:(c + 1) * CH]),
                              dbg_dump("sV", sv, pt[:], dbg_t["dP"][g, :, c * CH:(c + 1) * CH])]
                rD.set_free(gidx, [("sV", sv)] + ([dbg_idg_conds[(c, "dec", g)]] if dbg else []))
                at_ = rAs.acquire(P, "vector", gidx)
                bt_ = rBs.acquire(P, "vector", gidx)
                P.wait("vector", "sS", ev_evac[(c, "gate", g)][0])
                va_ = P.op("vector", lambda e, o_=at_[:], l_=lt[:], g2_=gt_[:]:
                           e.tensor_tensor(out=o_, in0=l_, in1=g2_, op=ALU.mult), "sV")
                vb_ = P.op("vector", lambda e, o_=bt_[:], p_=pt[:], g2_=gt_[:]:
                           e.tensor_tensor(out=o_, in0=p_, in1=g2_, op=ALU.mult), "sV")
                sa = f"dSpA{gidx % 4}"
                sb = f"dSpB{gidx % 4}"
                P.wait("gpsimd", "sV", vb_)
                spa = P.op("gpsimd", lambda e, s_=at_[:],
                           d_=sca.ap()[g, :, c * CH:(c + 1) * CH]:
                           e.dma_start(out=d_, in_=s_), sa, 16)
                spb = P.op("gpsimd", lambda e, s_=bt_[:],
                           d_=scb.ap()[g, :, c * CH:(c + 1) * CH]:
                           e.dma_start(out=d_, in_=s_), sb, 16)
                ev_spill[("a", c, g)] = (sa, spa)
                ev_spill[("b", c, g)] = (sb, spb)
                rAs.set_free(gidx, [(sa, spa)])
                rBs.set_free(gidx, [(sb, spb)])
                rG.set_free(gidx, [("sV", vb_)] + ([dbg_idg_conds[(c, "gate", g)]] if dbg else []))
                rL.set_free(gidx, [("sV", va_), ("sS", cv)] + dbg_lp[:1])
                rP.set_free(gidx, [("sV", vb_), ("sS", cv)] + dbg_lp[1:])

    P.wait("gpsimd", "sS", ev_carry[(nchunk - 1, G - 1)])
    agv = P.op("gpsimd", lambda e: e.dma_start(out=ag_in.ap()[:, :], in_=bnd[:]), "dAgIn", 16)
    P.wait("gpsimd", "dAgIn", agv)
    P.op("gpsimd", lambda e: e.collective_compute(
        "AllGather", ALU.bypass, replica_groups=[list(range(n_ranks))],
        ins=[ag_in.ap().opt()], outs=[ag_out.ap().opt()]).then_inc(P.hs["sCC"]))
    P.ops["gpsimd"].append(lambda e: e.wait_ge(P.hs["sCC"], 1))
    gl = None
    for j in range(n_ranks):
        gl = P.op("gpsimd", lambda e, o_=gath[:, j * 32:(j + 1) * 32],
                  s_=ag_out.ap()[j * 128:(j + 1) * 128, :]:
                  e.dma_start(out=o_, in_=s_), "dAgOut", 16)
    P.wait("vector", "dAgOut", gl)
    cur = 0
    for j in range(n_ranks):
        Lj = gath[:, j * 32:j * 32 + 16]
        Pj = gath[:, j * 32 + 16:j * 32 + 32]
        P.op("vector", lambda e, P_=Pj, s_=sinP[cur][:]: e.tensor_tensor(
            out=sinT[0][:], in0=P_, in1=s_, op=ALU.mult))
        P.op("vector", lambda e, L_=Lj: e.tensor_tensor(
            out=sinT[0][:], in0=sinT[0][:], in1=L_, op=ALU.add))
        P.op("vector", lambda e, s_=sinP[cur][:]: e.tensor_tensor(
            out=sinT[1][:], in0=sinT[0][:], in1=s_, op=ALU.subtract))
        P.op("vector", lambda e, m_=selm[:, j:j + 1], s_=sinP[cur][:],
             o_=sinP[1 - cur][:]: e.scalar_tensor_tensor(
                 out=o_, in0=sinT[1][:], scalar=m_, in1=s_,
                 op0=ALU.mult, op1=ALU.add))
        cur = 1 - cur
    sv_comb = P.op("vector", lambda e: e.nop(), "sV")
    P.wait("scalar", "sV", sv_comb)
    ev_sinF = P.op("scalar", lambda e, i_=sinP[cur][:]: e.activation(
        out=sinF[:], in_=i_, func=AF.Identity), "sS")

    ev_y = {}; ev_D2 = {}; ev_z = {}; ev_zT = {}; ev_fin = {}
    ev_w2 = {}; ev_x2 = {}; ev_abld = {}; ev_sw2 = {}; ev_atld = {}; ev_C2 = {}
    rGl2 = Ring(rL.tiles)
    rA2 = Ring(rA2.tiles + rD.tiles[0:2] + rP.tiles[0:2])
    rB2 = Ring(rB2.tiles + rD.tiles[2:4] + rP.tiles[2:4])
    last_D_all = ev_D[(nchunk - 1, 3, "gate")]
    def emit_x2load(i):
        if i < 0 or i >= NXT:
            return
        x2t = rX2.acquire(P, "gpsimd", NXT + i)
        lsem = f"dX{i % 2}"
        lv = P.op("gpsimd", lambda e, d_=x2t[:], s_=x_d[i * 128:(i + 1) * 128, :]:
                  e.dma_start(out=d_, in_=s_), lsem, 16)
        ev_x2[i] = (lsem, lv, x2t)

    def emit_w2load(j, c):
        if j >= 32:
            return
        mq_, kp = j // 8, j % 8
        if c == 0 and j == 0:
            P.wait("sync", "dCv_out", ev_conv["out"])
        wt = wring["out"].acquire(P, "sync", c * 32 + j)
        lsem = f"dW_out{(c * 32 + j) % 2}"
        src = wsc["out"].ap()[mq_, kp, :, :]
        wv = P.op("sync", lambda e, d_=wt[:], s_=src: e.dma_start(
            out=d_, in_=s_), lsem, 16)
        for kk in range(2):
            ev_w2[(c, mq_, kp * 2 + kk)] = (lsem, wv, wt, c * 32 + j, kk)

    def emit_swload2(j, c):
        if j >= 32:
            return
        oq_, kp = j // 8, j % 8
        if c == 0 and j == 0:
            P.wait("sync", "dCv_sw", ev_conv["sw"])
        wt = swring.acquire(P, "sync", c * 32 + j)
        lsem = f"dWsw{(c * 32 + j) % 2}"
        src = swsc.ap()[oq_, kp, :, :]
        wv = P.op("sync", lambda e, d_=wt[:], s_=src: e.dma_start(
            out=d_, in_=s_), lsem, 16)
        for kk in range(2):
            ev_sw2[(c, oq_, kp * 2 + kk)] = (lsem, wv, wt, c * 32 + j, kk)

    for j in range(2):
        emit_swload2(j, 0)
        emit_w2load(j, 0)

    for s2 in ([f"dSpA{i}" for i in range(4)] + [f"dSpB{i}" for i in range(4)] +
               ["dSpT0", "dSpT1"]):
        P.wait("sync", s2, P.cnt[s2])
        P.wait("scalar", s2, P.cnt[s2])
        P.wait("gpsimd", s2, P.cnt[s2])

    emit_x2load(0)
    emit_x2load(1)

    def emit_abload_gi(gi):
        if gi >= nchunk * G:
            return
        c2, g = gi // G, gi % G
        at2 = rA2.acquire(P, "scalar", gi)
        bt2 = rB2.acquire(P, "gpsimd", gi)
        sa2 = f"dPA{gi % 6}"
        sb2 = f"dPB{gi % 6}"
        va = P.op("scalar", lambda e, d_=at2[:], s_=sca.ap()[g, :, c2 * CH:(c2 + 1) * CH]:
                  e.dma_start(out=d_, in_=s_), sa2, 16)
        vb = P.op("gpsimd", lambda e, d_=bt2[:], s_=scb.ap()[g, :, c2 * CH:(c2 + 1) * CH]:
                  e.dma_start(out=d_, in_=s_), sb2, 16)
        ev_abld[(c2, g)] = ((sa2, va), (sb2, vb), at2, bt2)

    for gi0 in range(6):
        emit_abload_gi(gi0)

    for c in range(nchunk):

        def emit_atload(c=c):
            if c > 0:
                P.wait("sync", "sT", ev_C2[(c - 1, 3)])
            vals = []
            for h in range(2):
                lsem = f"dPT{h}"
                src2 = scat.ap()[h * 8:(h + 1) * 8, :, c * CH:(c + 1) * CH]
                lv = P.op("sync", lambda e, d_=aT[:, h * 8 * CH:(h + 1) * 8 * CH].rearrange(
                    "p (g s) -> p g s", g=8), s_=src2.rearrange("g p s -> p g s"):
                    e.dma_start(out=d_, in_=s_), lsem, 16)
                vals.append((lsem, lv))
            return vals

        ev_atld[c] = emit_atload()
        if c > 0:
            for j in range(2):
                emit_swload2(j, c)
                emit_w2load(j, c)
        P.wait("vector", "sS", ev_sinF)
        if c == 0:
            P.wait("vector", "sT", last_D_all)
            P.wait("vector", "dCv_out", ev_conv["out"])
        for g in range(G):
            (sa2, va), (sb2, vb), at2, bt2 = ev_abld[(c, g)]
            P.wait("vector", sa2, va)
            P.wait("vector", sb2, vb)
            if c > 0:
                P.wait("vector", "sT", ev_D2[(c - 1, 3)])
            yv = P.op("vector", lambda e, a_=at2[:], b_=bt2[:], m_=sinF[:, g:g + 1],
                      o_=xnT[:, g * CH:(g + 1) * CH]:
                      e.scalar_tensor_tensor(out=o_, in0=b_, scalar=m_, in1=a_,
                                             op0=ALU.mult, op1=ALU.add), "sV")
            ev_y[(c, g)] = yv
            rA2.set_free(c * G + g, [("sV", yv)])
            rB2.set_free(c * G + g, [("sV", yv)])
            emit_abload_gi(c * G + g + 6)

        for h2, (ls2, lv2) in enumerate(ev_atld[c]):
            P.wait("tensor", ls2, lv2)
        grp2 = 0
        for q in range(4):
            lo = (grp2 % 2) * 4
            grp2 += 1
            bts = [bank_acq(lo + m4) for m4 in range(4)]
            tv = None
            for kt in range(G):
                lsem, wv, wt, wi, kk = ev_sw2[(c, q, kt)]
                P.wait("tensor", lsem, wv)
                for m4 in range(4):
                    tv = P.op("tensor", lambda e, bk=bts[m4],
                              w_=wt[:, kk * CH + m4 * 128:kk * CH + (m4 + 1) * 128],
                              r_=aT[:, kt * CH:(kt + 1) * CH], k_=kt:
                              e.matmul(out=bk[:, 0:512], lhsT=w_, rhs=r_,
                                       start=(k_ == 0), stop=(k_ == G - 1)),
                              "sT" if m4 == 3 else None)
                if kk == 1:
                    swring.set_free(wi, [("sT", tv)])
                    emit_swload2((q * G + kt + 1) // 2 + 1, c)
            ev_C2[(c, q)] = tv
            P.wait("scalar", "sT", tv)
            gq = []
            for m4 in range(4):
                o = q * 4 + m4
                gt = rGl2.acquire(P, "scalar", c * G + o)
                gv = P.op("scalar", lambda e, bk=bts[m4], o_=o, g_=gt[:]: e.activation(
                    out=g_, in_=bk[:, 0:512], func=AF.Silu,
                    scale=pcol("alpha", o_), bias=pcol("alphabias", o_)), "sS")
                gq.append((gv, gt))
                bank_rel(lo + m4, "sS", gv)
            lo = (grp2 % 2) * 4
            grp2 += 1
            bts = [bank_acq(lo + m4) for m4 in range(4)]
            tv = None
            for kt in range(G):
                lsem, wv, wt, wi, kk = ev_w2[(c, q, kt)]
                P.wait("tensor", lsem, wv)
                P.wait("tensor", "sV", ev_y[(c, kt)])
                for m4 in range(4):
                    tv = P.op("tensor", lambda e, bk=bts[m4],
                              w_=wt[:, kk * CH + m4 * 128:kk * CH + (m4 + 1) * 128],
                              r_=xnT[:, kt * CH:(kt + 1) * CH], k_=kt:
                              e.matmul(out=bk[:, 0:512], lhsT=w_, rhs=r_,
                                       start=(k_ == 0), stop=(k_ == G - 1)),
                              "sT" if m4 == 3 else None)
                if kk == 1:
                    wring["out"].set_free(wi, [("sT", tv)])
                    emit_w2load((q * G + kt + 1) // 2 + 1, c)
            ev_D2[(c, q)] = tv
            P.wait("scalar", "sT", tv)
            for m4 in range(4):
                m = q * 4 + m4
                st_ = rSs.acquire(P, "scalar", c * G + m)
                sv2 = P.op("scalar", lambda e, bk=bts[m4], m_=m, o_=st_[:]:
                           e.activation(out=o_, in_=bk[:, 0:512], func=AF.Identity,
                                        scale=-1.0, bias=pcol("b_out", m_)), "sS")
                bank_rel(lo + m4, "sS", sv2)
                gv, gt2 = gq[m4]
                P.wait("vector", "sS", max(sv2, gv))
                if c > 0:
                    P.wait("vector", "sT", ev_zT[(c - 1, TPC - 1, 3)])
                zv = P.op("vector", lambda e, s_=st_[:], g_=gt2[:],
                          o_=zt[:, m * CH:(m + 1) * CH]: e.tensor_tensor(
                              out=o_, in0=s_, in1=g_, op=ALU.mult), "sV")
                ev_z[(c, m)] = zv
                rSs.set_free(c * G + m, [("sV", zv)])
                rGl2.set_free(c * G + m, [("sV", zv)])
        for t in range(TPC):
            ot = rOut.acquire(P, "vector", c * TPC + t)
            x2sem, lxv, x2t = ev_x2[c * TPC + t]
            for j in range(4):
                P.wait("tensor", "sV", ev_z[(c, 4 * j + 3)])
                b = (t % 2) * 4 + j
                bt = bank_acq(b)
                tv = None
                for qq in range(4):
                    g = 4 * j + qq
                    tv = P.op("tensor", lambda e, bk=bt, q_=qq,
                              s_=zt[:, g * CH + t * 128:g * CH + (t + 1) * 128]:
                              e.transpose(out=bk[:, q_ * 128:(q_ + 1) * 128],
                                          in_=s_, identity=ident[:]),
                              "sT" if qq == 3 else None)
                ev_zT[(c, t, j)] = tv
                P.wait("vector", "sT", tv)
                P.wait("vector", x2sem, lxv)
                fv = P.op("vector", lambda e, bk=bt, x_=x2t[:, j * 512:(j + 1) * 512],
                          o_=ot[:, j * 512:(j + 1) * 512]: e.tensor_tensor(
                              out=o_, in0=x_, in1=bk[:, 0:512], op=ALU.add), "sV")
                ev_fin[(c, t, j)] = fv
                bank_rel(b, "sV", fv)
            rX2.set_free(NXT + c * TPC + t, [("sV", ev_fin[(c, t, 3)])])
            emit_x2load(c * TPC + t + 2)
            P.wait("gpsimd", "sV", ev_fin[(c, t, 3)])
            idx = c * TPC + t
            osem = f"dOut{(c * TPC + t) % 2}"
            ov = P.op("gpsimd", lambda e, s_=ot[:], d_=out_d[idx * 128:(idx + 1) * 128, :]:
                      e.dma_start(out=d_, in_=s_), osem, 16)
            rOut.set_free(c * TPC + t, [(osem, ov)])

    P.wait("gpsimd", "dOut0", P.cnt["dOut0"])
    P.wait("gpsimd", "dOut1", P.cnt["dOut1"])

    with stack, nc.Block() as block:
        block.sync(lambda e: P.run("sync", e))
        block.scalar(lambda e: P.run("scalar", e))
        block.vector(lambda e: P.run("vector", e))
        block.tensor(lambda e: P.run("tensor", e))
        block.gpsimd(lambda e: P.run("gpsimd", e))
    nc._plan_trace = P.trace
    return nc


def _core_inputs(inputs, core, n_ranks=NCORE):
    b, q = core // 4, core % 4
    x = np.asarray(inputs["L0_attn_x"], np.float32)
    m = {
        "x": np.ascontiguousarray(x[b, q * 4096:(q + 1) * 4096, :]),
        "w_in": np.asarray(inputs["L0_attn_W_in"], np.float32),
        "w_dec": np.asarray(inputs["L0_attn_W_decay"], np.float32),
        "w_gate": np.asarray(inputs["L0_attn_W_gate"], np.float32),
        "w_out": np.asarray(inputs["L0_attn_W_out"], np.float32),
        "ffnw": np.asarray(inputs["L0_ffn_weight"], np.float32),
        "b_in": np.asarray(inputs["L0_attn_b_in"], np.float32),
        "b_dec": np.asarray(inputs["L0_attn_b_decay"], np.float32),
        "b_gate": np.asarray(inputs["L0_attn_b_gate"], np.float32),
        "b_out": np.asarray(inputs["L0_attn_b_out"], np.float32),
        "dbias": np.asarray(inputs["L0_attn_decay_bias"], np.float32),
        "gamma": np.asarray(inputs["L0_ffn_gamma"], np.float32),
        "beta": np.asarray(inputs["L0_ffn_beta"], np.float32),
        "ffnb": np.asarray(inputs["L0_ffn_bias"], np.float32),
        "alpha": np.asarray(inputs["L0_ffn_alpha"], np.float32),
        "ident": np.eye(128, dtype=np.float32),
    }
    sel = np.zeros(8, np.float32)
    for j in range(n_ranks):
        if j // 4 == b and j % 4 < q:
            sel[j] = 1.0
    m["selm"] = np.tile(sel[None, :], (128, 1)).astype(np.float32)
    return m


def kernel(**inputs):
    nc = build_program(nchunk=8, n_ranks=NCORE)
    in_maps = [_core_inputs(inputs, c) for c in range(NCORE)]
    res = run_bass_kernel_spmd(nc, in_maps, core_ids=list(range(NCORE)))
    out = np.zeros((2, 16384, 2048), np.float32)
    for c in range(NCORE):
        b, q = c // 4, c % 4
        out[b, q * 4096:(q + 1) * 4096, :] = res.results[c]["out"]
    return out


# version 16
# speedup vs baseline: 1.2280x; 1.1884x over previous
import numpy as np
import concourse.bass as bass
import concourse.mybir as mybir
from concourse.bass_utils import run_bass_kernel_spmd
from concourse.replica_groups import maybe_share_collective_output_space
from contextlib import ExitStack

F32 = mybir.dt.float32
BF16 = mybir.dt.bfloat16
FP8 = mybir.dt.float8e4
AF = mybir.ActivationFunctionType
ALU = mybir.AluOpType
AX = mybir.AxisListType

D = 2048
G = 16
CH = 512
TPC = 4
NCORE = 8
INV_D = 1.0 / D
EPS = 1e-5
ENGS = ("sync", "scalar", "vector", "tensor", "gpsimd")
SEM_OWNER = {"sV": "vector", "sS": "scalar", "sT": "tensor", "sG": "gpsimd"}
FAMS = ("in", "dec", "gate")
FAM_AF = {"in": AF.Silu, "dec": AF.Sigmoid, "gate": AF.Sigmoid}
FAM_BIAS = {"in": "b_in", "dec": "bdd", "gate": "b_gate"}


class Plan:
    def __init__(self, nc):
        self.nc = nc
        self.ops = {e: [] for e in ENGS}
        self.hs = {}
        self.cnt = {}
        self.last_wait = {}
        self.trace = []

    def new_sem(self, stack, name):
        self.hs[name] = stack.enter_context(self.nc.semaphore(name))
        self.cnt[name] = 0

    def wait(self, eng, name, val):
        if val is None or val <= 0:
            return
        own = SEM_OWNER.get(name)
        if own == eng:
            assert val <= self.cnt[name], f"own-engine future wait {eng} {name} {val}"
            return
        key = (eng, name)
        if self.last_wait.get(key, 0) >= val:
            return
        self.last_wait[key] = val
        self.trace.append((eng, "wait", name, val))
        h = self.hs[name]
        self.ops[eng].append(lambda e, h=h, v=val: e.wait_ge(h, v))

    def op(self, eng, build, inc=None, incn=1):
        import traceback
        ln = traceback.extract_stack()[-2].lineno
        self.trace.append((eng, "op", inc, self.cnt.get(inc, 0) + incn if inc else None, ln))
        if inc is not None:
            self.cnt[inc] += incn
            after = self.cnt[inc]
            h = self.hs[inc]
            self.ops[eng].append(lambda e, b=build, h=h, n=incn: b(e).then_inc(h, n))
            return after
        self.ops[eng].append(lambda e, b=build: b(e))
        return None

    def run(self, eng_name, eng):
        for fn in self.ops[eng_name]:
            fn(eng)


class Ring:
    def __init__(self, tiles):
        self.tiles = tiles
        self.n = len(tiles)
        self.free = [[] for _ in range(self.n)]

    def acquire(self, P, eng, idx):
        s = idx % self.n
        for sem, val in self.free[s]:
            P.wait(eng, sem, val)
        self.free[s] = []
        return self.tiles[s]

    def set_free(self, idx, conds):
        self.free[idx % self.n] = [c for c in conds if c[1]]


def build_program(nchunk=8, n_ranks=NCORE, dbg=False, phase1_only=False, skip_c=False, xnt_input=False):
    nc = bass.Bass("TRN2", target_bir_lowering=False)
    S = nchunk * CH

    x_d = nc.dram_tensor("x", [S, D], F32, kind="ExternalInput").ap()
    fam_src = {}
    for f, nm in (("in", "w_in"), ("dec", "w_dec"), ("gate", "w_gate"), ("out", "w_out")):
        fam_src[f] = nc.dram_tensor(nm, [D, D], F32, kind="ExternalInput").ap()
    ffnw_d = nc.dram_tensor("ffnw", [D, D], F32, kind="ExternalInput").ap()
    VNAMES = ["b_in", "b_dec", "b_gate", "b_out", "dbias", "gamma", "beta", "ffnb", "alpha"]
    vecs_d = {vn: nc.dram_tensor(vn, [D], F32, kind="ExternalInput").ap() for vn in VNAMES}
    ident_d = nc.dram_tensor("ident", [128, 128], F32, kind="ExternalInput").ap()
    xnt_d = (nc.dram_tensor("xnt_in", [nchunk, G * 128, CH], BF16, kind="ExternalInput").ap()
             if xnt_input else None)
    selm_d = nc.dram_tensor("selm", [128, 8], F32, kind="ExternalInput").ap()
    out_d = nc.dram_tensor("out", [S, D], F32, kind="ExternalOutput").ap()

    kd = "ExternalOutput" if dbg else "Internal"
    wsc = {f: nc.dram_tensor(f"wsc_{f}", [4, 8, 128, 1024], BF16, kind=kd) for f in ("in", "dec", "gate", "out")}
    swsc = nc.dram_tensor("swsc", [4, 8, 128, 1024], FP8, kind=kd)
    sca = nc.dram_tensor("sca", [G, 128, S], F32, kind=kd)
    scat = nc.dram_tensor("scat", [G, 128, S], FP8, kind=kd)
    scb = nc.dram_tensor("scb", [G, 128, S], F32, kind=kd)
    ag_space = maybe_share_collective_output_space(
        "AllGather", [list(range(n_ranks))], prefer_local=True)
    ag_in = nc.dram_tensor("ag_in", [128, 32], F32)
    if dbg:
        dbg_t = {
            "dXN": nc.dram_tensor("dXN", [S, D], F32, kind="ExternalOutput").ap(),
            "dQ": nc.dram_tensor("dQ", [S, D], F32, kind="ExternalOutput").ap(),
            "dXNT": nc.dram_tensor("dXNT", [G, 128, S], BF16, kind="ExternalOutput").ap(),
            "dAT": nc.dram_tensor("dAT", [G, 128, S], FP8, kind="ExternalOutput").ap(),
            "dI": nc.dram_tensor("dI", [G, 128, S], F32, kind="ExternalOutput").ap(),
            "dD": nc.dram_tensor("dD", [G, 128, S], F32, kind="ExternalOutput").ap(),
            "dG": nc.dram_tensor("dG", [G, 128, S], F32, kind="ExternalOutput").ap(),
            "dL": nc.dram_tensor("dL", [G, 128, S], F32, kind="ExternalOutput").ap(),
            "dP": nc.dram_tensor("dP", [G, 128, S], F32, kind="ExternalOutput").ap(),
            "dST": nc.dram_tensor("dST", [nchunk * TPC, 128, 48], F32, kind="ExternalOutput").ap(),
            "dPV": nc.dram_tensor("dPV", [128, 16 * 11], F32, kind="ExternalOutput").ap(),
            "dAS": nc.dram_tensor("dAS", [G, 128, S], F32, kind="ExternalOutput").ap(),
            "dSWF": nc.dram_tensor("dSWF", [G, 4, 128, 512], F32, kind="ExternalOutput").ap(),
            "dRM": nc.dram_tensor("dRM", [G, 128, 8], F32, kind="ExternalOutput").ap(),
        }
    ag_out = nc.dram_tensor("ag_out", [128 * n_ranks, 32], F32, addr_space=ag_space)

    stack = ExitStack()
    sb = lambda name, shape, dt: stack.enter_context(nc.sbuf_tensor(name, shape, dt))

    ident = sb("identsb", [128, 128], F32)
    SLOT = {n: i for i, n in enumerate(VNAMES + ["bdd", "alphabias"])}
    pvec = sb("pvec", [128, 16 * len(SLOT)], F32)
    selm = sb("selmsb", [128, 8], F32)
    epsc = sb("epsc", [128, 1], F32)
    zeros = sb("zerosc", [128, CH], F32)
    bnd = sb("bnd", [128, 32], F32)
    gath = sb("gath", [128, 32 * n_ranks], F32)
    sinP = [sb(f"sin{i}", [128, 16], F32) for i in range(2)]
    sinT = [sb(f"sint{i}", [128, 16], F32) for i in range(2)]
    sinF = sb("sinF", [128, 16], F32)
    statc = sb("statc", [128, 48 * TPC], F32)

    xr = Ring([sb(f"xr{i}", [128, D], F32) for i in range(2)])
    xnr = Ring([sb(f"xnr{i}", [128, D], F32) for i in range(4)])
    qr = Ring([sb(f"qr{i}", [128, D], F32) for i in range(2)])
    xnT = sb("xnT", [128, G * CH], BF16)
    aT = sb("aT", [128, G * CH], FP8)
    rI = Ring([sb(f"ii{i}", [128, CH], F32) for i in range(4)])
    rD = Ring([sb(f"dd{i}", [128, CH], F32) for i in range(4)])
    rG = Ring([sb(f"gg{i}", [128, CH], F32) for i in range(4)])
    rU = Ring([sb(f"uu{i}", [128, CH], F32) for i in range(4)])
    rL = Ring([sb(f"ll{i}", [128, CH], F32) for i in range(4)])
    rP = Ring([sb(f"pp{i}", [128, CH], F32) for i in range(4)])
    rAs = Ring([sb(f"ast{i}", [128, CH], F32) for i in range(2)])
    rBs = Ring([sb(f"bst{i}", [128, CH], F32) for i in range(2)])
    wring = {f: Ring([sb(f"w_{f}_{i}", [128, 2 * CH], BF16) for i in range(3)])
             for f in ("in", "dec", "gate", "out")}
    swring = Ring([sb(f"swr{i}", [128, 2 * CH], FP8) for i in range(2)])
    rA2, rB2, rX2, rOut, rSs = rAs, rBs, xr, xnr, rI

    prostack = ExitStack()
    psb = lambda name, shape, dt: prostack.enter_context(nc.sbuf_tensor(name, shape, dt))
    convfV = [psb(f"convfV{i}", [128, 512], F32) for i in range(4)]
    convbV = [psb(f"convbV{i}", [128, 512], BF16) for i in range(4)]
    convfS = [psb(f"convfS{i}", [128, 512], F32) for i in range(2)]
    convbS = [psb(f"convbS{i}", [128, 512], BF16) for i in range(2)]
    sw8 = [psb("sw8_0", [128, D], FP8)]
    swf = [psb(f"swf{i}", [128, 512], F32) for i in range(2)]
    swin = [psb("swin0", [128, D], F32)]
    rowm = psb("rowm", [128, 8], F32)
    prostack.close()
    zt = sb("zt", [128, G * CH], F32)

    pb = [stack.enter_context(nc.psum_tensor(f"pb{i}", [128, 512], F32)) for i in range(8)]
    bank_free = [[] for _ in range(8)]

    P = Plan(nc)
    sem_names = ["dPro", "dAgIn", "dAgOut", "dOut0", "dOut1", "dDbg",
                 "sV", "sS", "sT", "sG", "sCC"]
    sem_names += [f"dX{i}" for i in range(3)]
    for f in ("in", "dec", "gate", "out"):
        sem_names += [f"dW_{f}{i}" for i in range(4)]
        sem_names.append(f"dCv_{f}")
    sem_names += [f"dWsw{i}" for i in range(4)] + ["dCv_sw"]
    sem_names += [f"dSpA{i}" for i in range(4)] + [f"dSpB{i}" for i in range(4)]
    sem_names += [f"dSpC{i}" for i in range(4)] + [f"dSpG{i}" for i in range(2)]
    sem_names += [f"dPA{i}" for i in range(6)] + [f"dPB{i}" for i in range(6)]
    sem_names += [f"dPC{i}" for i in range(6)] + [f"dPG{i}" for i in range(2)]
    sem_names += ["dPLV0", "dPLV1", "dPLV2", "dPLV3", "dPLS0", "dPLS1", "dSwin0", "dSwin1"]
    sem_names += ["dSpT0", "dSpT1", "dPT0", "dPT1"]
    for s_ in sem_names:
        P.new_sem(stack, s_)

    def bank_acq(i):
        for sem, val in bank_free[i]:
            P.wait("tensor", sem, val)
        bank_free[i] = []
        return pb[i]

    def bank_rel(i, sem, val):
        bank_free[i] = [(sem, val)]

    def dbg_dump(wait_sem, wait_val, src_ap, dst_ap):
        P.wait("gpsimd", wait_sem, wait_val)
        v = P.op("gpsimd", lambda e, d_=dst_ap, s_=src_ap: e.dma_start(
            out=d_, in_=s_), "dDbg", 16)
        return ("dDbg", v)

    def pcol(slot, g):
        c = SLOT[slot] * 16 + g
        return pvec[:, c:c + 1]

    stat_cols = {"meanv": 32, "varv": 33, "mean1": 34, "lnv1": 35, "rs1": 36,
                 "mu2v": 37, "var2v": 38, "mu2": 39, "lnv2": 40, "rs2": 41,
                 "tmp0": 42, "tmp1": 43, "tmp2": 44, "msq": 45}

    def scol(name, t):
        c = t * 48 + stat_cols[name]
        return statc[:, c:c + 1]

    def bn6(t):
        return statc[:, t * 48:t * 48 + 24]

    def mvc(t, i):
        c = t * 48 + 24 + i
        return statc[:, c:c + 1]

    def emit_bn_stats(xt, t, mean_name, var_name):
        assert stat_cols[var_name] == stat_cols[mean_name] + 1
        for b in range(4):
            P.op("vector", lambda e, x_=xt[:, b * 512:(b + 1) * 512],
                 o_=bn6(t)[:, b * 6:(b + 1) * 6]: e.bn_stats(out=o_, in_=x_))
        for k in range(6):
            P.op("vector", lambda e, o_=scol("tmp0", t): e.tensor_copy(
                out=o_, in_=epsc[:]))
        return P.op("vector", lambda e, i_=bn6(t),
                    o_=statc[:, t * 48 + stat_cols[mean_name]:t * 48 + stat_cols[mean_name] + 2]:
                    e.bn_aggr(out=o_, in_=i_), "sV")

    P.op("sync", lambda e: e.dma_start(out=ident[:], in_=ident_d[:, :]), "dPro", 16)
    P.op("sync", lambda e: e.dma_start(out=selm[:], in_=selm_d[:, :]), "dPro", 16)
    for vn in VNAMES:
        dst = pvec[:, SLOT[vn] * 16:SLOT[vn] * 16 + 16]
        src = vecs_d[vn].rearrange("(g p) -> p g", p=128)
        P.op("sync", lambda e, d_=dst, s_=src: e.dma_start(
            out=d_, in_=s_, allow_slow_non_contiguous=True), "dPro", 16)
    ev_pro_loads = P.cnt["dPro"]

    P.op("gpsimd", lambda e: e.memset(epsc[:], EPS))
    P.op("gpsimd", lambda e: e.memset(zeros[:], 0.0))
    P.op("gpsimd", lambda e: e.memset(bnd[:, 0:16], 0.0))
    P.op("gpsimd", lambda e: e.memset(bnd[:, 16:32], 1.0))
    ev_consts = P.op("gpsimd", lambda e: e.memset(sinP[0][:], 0.0), "sG")

    P.wait("vector", "dPro", ev_pro_loads)
    P.op("vector", lambda e: e.tensor_tensor(
        out=pvec[:, SLOT["bdd"] * 16:SLOT["bdd"] * 16 + 16],
        in0=pvec[:, SLOT["b_dec"] * 16:SLOT["b_dec"] * 16 + 16],
        in1=pvec[:, SLOT["dbias"] * 16:SLOT["dbias"] * 16 + 16], op=ALU.add))
    ev_pvec = P.op("vector", lambda e: e.tensor_tensor(
        out=pvec[:, SLOT["alphabias"] * 16:SLOT["alphabias"] * 16 + 16],
        in0=pvec[:, SLOT["alpha"] * 16:SLOT["alpha"] * 16 + 16],
        in1=pvec[:, SLOT["ffnb"] * 16:SLOT["ffnb"] * 16 + 16], op=ALU.mult), "sV")

    conv_ring = {"vector": Ring(list(zip(convfV, convbV))),
                 "scalar": Ring(list(zip(convfS, convbS)))}
    conv_nslot = {"vector": 4, "scalar": 2}
    conv_idx = {"vector": 0, "scalar": 0}
    ev_conv = {}

    def convert_tile(fam, eng, kt, h):
        esem = "sV" if eng == "vector" else "sS"
        i = conv_idx[eng]
        conv_idx[eng] += 1
        cf, cb = conv_ring[eng].acquire(P, "sync", i)
        lsem = ("dPLV" + str(i % 4)) if eng == "vector" else ("dPLS" + str(i % 2))
        src = fam_src[fam][kt * 128:(kt + 1) * 128, h * 512:(h + 1) * 512]
        ldv = P.op("sync", lambda e, d_=cf[:], s_=src: e.dma_start(
            out=d_, in_=s_), lsem, 16)
        P.wait(eng, lsem, ldv)
        if eng == "vector":
            cv = P.op(eng, lambda e, o=cb[:], i_=cf[:]: e.tensor_copy(
                out=o, in_=i_), esem)
        else:
            cv = P.op(eng, lambda e, o=cb[:], i_=cf[:]: e.activation(
                out=o, in_=i_, func=AF.Identity), esem)
        P.wait("gpsimd", esem, cv)
        dst = wsc[fam].ap()[h, kt // 2, :, (kt % 2) * 512:(kt % 2) * 512 + 512]
        stv = P.op("gpsimd", lambda e, d_=dst, s_=cb[:]: e.dma_start(
            out=d_, in_=s_), "dCv_" + fam, 16)
        conv_ring[eng].set_free(i, [(esem, cv), ("dCv_" + fam, stv)])

    def convert_fam(fam, eng, kts=None):
        for kt in (range(G) if kts is None else kts):
            for h in range(4):
                convert_tile(fam, eng, kt, h)
        ev_conv[fam] = P.cnt["dCv_" + fam]

    _swg = {"swf_free": [[], []], "sw8_free": [[], []], "swin_free": [[], []]}

    def emit_swgen_ot(ot):
        swf_free = _swg["swf_free"]
        sw8_free = _swg["sw8_free"]
        swin_free = _swg["swin_free"]
        if True:
            slot = 0
            sl2 = 0
            for sem, val in swin_free[sl2]:
                P.wait("sync", sem, val)
            swin_free[sl2] = []
            ldv = P.op("sync", lambda e, d_=swin[sl2][:], s_=ffnw_d[ot * 128:(ot + 1) * 128, :]:
                       e.dma_start(out=d_, in_=s_), "dSwin" + str(sl2), 16)
            P.wait("vector", "dSwin" + str(sl2), ldv)
            rsv = P.op("vector", lambda e, s_=swin[sl2][:]: e.reduce_sum(
                out=rowm[:, 4 * (ot % 2):4 * (ot % 2) + 1], in_=s_, axis=AX.X), "sV")
            P.wait("scalar", "sV", rsv)
            nmS = P.op("scalar", lambda e: e.activation(out=rowm[:, 4 * (ot % 2) + 1:4 * (ot % 2) + 2], in_=rowm[:, 4 * (ot % 2):4 * (ot % 2) + 1],
                                                        func=AF.Identity, scale=-INV_D), "sS")
            P.wait("vector", "sS", nmS)
            nmV = P.op("vector", lambda e: e.tensor_copy(out=rowm[:, 4 * (ot % 2) + 2:4 * (ot % 2) + 3], in_=rowm[:, 4 * (ot % 2) + 1:4 * (ot % 2) + 2]), "sV")
            P.wait("scalar", "sV", nmV)
            for sem, val in sw8_free[slot]:
                P.wait("vector", sem, val)
            sw8_free[slot] = []
            sgs = []
            for h in range(4):
                for sem, val in swf_free[h % 2]:
                    P.wait("scalar", sem, val)
                swf_free[h % 2] = []
                sg = P.op("scalar", lambda e, o_=swf[h % 2][:],
                          i_=swin[sl2][:, h * 512:(h + 1) * 512]: e.activation(
                              out=o_, in_=i_, func=AF.Sign, bias=rowm[:, 4 * (ot % 2) + 2:4 * (ot % 2) + 3]), "sS")
                sgs.append(sg)
                if dbg:
                    dbg_dump("sS", sg, swf[h % 2][:], dbg_t["dSWF"][ot, h, :, :])
                    dbg_dump("sS", sg, rowm[:, 0:8], dbg_t["dRM"][ot, :, :])
                P.wait("tensor", "sS", sg)
                tvs = []
                for it in range(4):
                    b = (h * 4 + it) % 8
                    bt = bank_acq(b)
                    tv = P.op("tensor", lambda e, bk=bt, s_=swf[h % 2][:, it * 128:(it + 1) * 128]:
                              e.transpose(out=bk[:, 0:128], in_=s_, identity=ident[:]), "sT")
                    P.wait("vector", "sT", tv)
                    itg = h * 4 + it
                    jcol = (itg % 2) * 8 + itg // 2
                    ev = P.op("vector", lambda e, bk=bt, o_=sw8[slot][:, jcol * 128:(jcol + 1) * 128]:
                              e.tensor_copy(out=o_, in_=bk[:, 0:128]), "sV")
                    bank_rel(b, "sV", ev)
                    tvs.append(tv)
                swf_free[h % 2] = [("sT", tvs[-1])]
                if dbg:
                    swf_free[h % 2].append(("dDbg", P.cnt["dDbg"]))
            last_ev = P.cnt["sV"]
            swin_free[sl2] = [("sS", sgs[-1]), ("sV", rsv)]
            P.wait("gpsimd", "sV", last_ev)
            stv = None
            oq_, oc_ = ot // 4, (ot % 4) * 128
            for half in range(2):
                for h2 in range(2):
                    dst = swsc.ap()[oq_, half * 4:(half + 1) * 4, :,
                                    h2 * 512 + oc_:h2 * 512 + oc_ + 128]
                    dst = dst.rearrange("it p o -> p it o")
                    lo_ = (h2 * 8 + half * 4) * 128
                    src = sw8[slot][:, lo_:lo_ + 512].rearrange(
                        "p (it o) -> p it o", it=4)
                    stv = P.op("gpsimd", lambda e, d_=dst, s_=src: e.dma_start(
                        out=d_, in_=s_), "dCv_sw", 16)
            sw8_free[slot] = [("dCv_sw", stv)]
            if ot == G - 1:
                ev_conv["sw"] = P.cnt["dCv_sw"]


    ev_xload = {}; ev_xn = {}; ev_q = {}; ev_r2 = {}; ev_mu2 = {}
    ev_A = {}; ev_Aev = {}; ev_B = {}; ev_sign = {}
    ev_C = {}; ev_gffn = {}; ev_D = {}; ev_evac = {}
    ev_u = {}; ev_scan = {}; ev_ab = {}; ev_carry = {}
    ev_spill = {}; ev_wtile = {}

    P.wait("scalar", "sG", ev_consts)
    P.wait("vector", "sG", ev_consts)
    P.wait("scalar", "dPro", ev_pro_loads)
    P.wait("scalar", "sV", ev_pvec)
    if dbg:
        dbg_dump("sV", ev_pvec, pvec[:, 0:16 * 11], dbg_t["dPV"][:, :])
    dbg_xn_conds = {}; dbg_q_conds = {}; dbg_idg_conds = {}

    NXT = nchunk * TPC

    def emit_xload(i):
        if i < 0 or i >= NXT:
            return
        xt = xr.acquire(P, "sync", i)
        lsem = f"dX{i % 2}"
        lv = P.op("sync", lambda e, d_=xt[:], s_=x_d[i * 128:(i + 1) * 128, :]:
                  e.dma_start(out=d_, in_=s_), lsem, 16)
        ev_xload[i] = (lsem, lv, xt)

    emit_xload(0)
    emit_xload(1)

    def emit_stats(c, only_t=None):
        for t in (range(TPC) if only_t is None else [only_t]):
            idx = c * TPC + t
            lsem, lv, xt = ev_xload[idx]
            P.wait("vector", lsem, lv)
            if c > 0:
                P.wait("vector", "sS", ev_r2[(c - 1, t)])
            v1 = emit_bn_stats(xt, t, "meanv", "varv")
            P.wait("scalar", "sV", v1)
            m1 = P.op("scalar", lambda e, i_=scol("meanv", t), o_=scol("mean1", t):
                      e.activation(out=o_, in_=i_, func=AF.Identity), "sS")
            P.op("scalar", lambda e, i_=scol("varv", t), o_=scol("lnv1", t):
                 e.activation(out=o_, in_=i_, func=AF.Ln, bias=epsc[:]))
            r1 = P.op("scalar", lambda e, i_=scol("lnv1", t), o_=scol("rs1", t):
                      e.activation(out=o_, in_=i_, func=AF.Exp, scale=-0.5), "sS")
            xnt = xnr.acquire(P, "vector", idx)
            P.wait("vector", "sS", r1)
            xnv = P.op("vector", lambda e, x_=xt[:], o_=xnt[:], m_=scol("mean1", t),
                       r_=scol("rs1", t): e.tensor_scalar(
                           out=o_, in0=x_, scalar1=m_, scalar2=r_,
                           op0=ALU.subtract, op1=ALU.mult), "sV")
            ev_xn[(c, t)] = (xnv, xnt)
            xr.set_free(idx, [("sV", xnv)])
            emit_xload(idx + 2)
            v2 = emit_bn_stats(xnt, t, "mu2v", "var2v")
            P.wait("scalar", "sV", v2)
            m2 = P.op("scalar", lambda e, i_=scol("mu2v", t), o_=scol("mu2", t):
                      e.activation(out=o_, in_=i_, func=AF.Identity), "sS")
            P.op("scalar", lambda e, i_=scol("var2v", t), o_=scol("lnv2", t):
                 e.activation(out=o_, in_=i_, func=AF.Ln, bias=epsc[:]))
            r2 = P.op("scalar", lambda e, i_=scol("lnv2", t), o_=scol("rs2", t):
                      e.activation(out=o_, in_=i_, func=AF.Exp, scale=-0.5), "sS")
            ev_r2[(c, t)] = r2
            ev_mu2[(c, t)] = max(m2, r2)
            if dbg:
                dbg_dump("sS", r2, statc[:, t * 48:(t + 1) * 48],
                         dbg_t["dST"][c * TPC + t, :, :])
                dbg_xn_conds[(c, t)] = dbg_dump(
                    "sV", xnv, xnt[:], dbg_t["dXN"][idx * 128:(idx + 1) * 128, :])


    emit_stats(0)

    for _kt in range(G):
        for _h in range(4):
            convert_tile("in", "vector", _kt, _h)
            convert_tile("dec", "scalar", _kt, _h)
            convert_tile("gate", "vector", _kt, _h)
    for _f in ("in", "dec", "gate"):
        ev_conv[_f] = P.cnt["dCv_" + _f]

    for c in range(nchunk):
        if xnt_input:
            if c > 0:
                P.wait("sync", "sT", ev_D[(c - 1, 3, "gate")])
            lv = P.op("sync", lambda e, d_=xnT[:].rearrange("p (k s) -> p k s", k=G),
                      s_=xnt_d[c].rearrange("(k p) s -> p k s", p=128):
                      e.dma_start(out=d_, in_=s_), "dAgIn", 16)
            fake = P.op("vector", lambda e, v=lv: e.nop(), "sV")
            P.wait("vector", "dAgIn", lv)
            for t in range(TPC):
                ev_Aev[(c, t, 3)] = fake
                ev_A[(c, t, 3)] = P.cnt["sT"]
                ev_B[(c, t, 3)] = P.cnt["sT"]
                xnv, xnt = ev_xn[(c, t)]
                xnr.set_free(c * TPC + t, [("sV", fake)] +
                             ([dbg_xn_conds[(c, t)]] if dbg else []))
        if (not xnt_input) and c > 0:
            for sem2, val2 in ev_at_free_prev:
                P.wait("scalar", sem2, val2)
        for t in range(TPC) if not xnt_input else []:
            xnv, xnt = ev_xn[(c, t)]
            qt = qr.acquire(P, "vector", c * TPC + t)
            P.wait("vector", "sS", ev_mu2[(c, t)])
            qv = P.op("vector", lambda e, x_=xnt[:], o_=qt[:], m_=scol("mu2", t),
                      r_=scol("rs2", t): e.tensor_scalar(
                          out=o_, in0=x_, scalar1=m_, scalar2=r_,
                          op0=ALU.subtract, op1=ALU.mult), "sV")
            ev_q[(c, t)] = (qv, qt)
            if dbg:
                idx = c * TPC + t
                dbg_q_conds[(c, t)] = dbg_dump(
                    "sV", qv, qt[:], dbg_t["dQ"][idx * 128:(idx + 1) * 128, :])
            if c + 1 < nchunk:
                emit_stats(c + 1, only_t=t)
            P.wait("tensor", "sV", xnv)
            loA = (t % 2) * 4
            for j in range(4):
                b = loA + j
                bt = bank_acq(b)
                tv = None
                for qq in range(4):
                    g = 4 * j + qq
                    tv = P.op("tensor", lambda e, bk=bt, s_=xnt[:, g * 128:(g + 1) * 128],
                              q_=qq: e.transpose(out=bk[:, q_ * 128:(q_ + 1) * 128],
                                                 in_=s_, identity=ident[:]),
                              "sT" if qq == 3 else None)
                ev_A[(c, t, j)] = tv
                P.wait("scalar", "sT", tv)
                if c > 0 and t == 0 and j == 0:
                    P.wait("scalar", "sT", ev_D[(c - 1, 3, "gate")])
                o_ap = xnT[:].rearrange("p (g s) -> p g s", g=G)[
                    :, 4 * j:4 * j + 4, t * 128:(t + 1) * 128]
                last = P.op("scalar", lambda e, bk=bt, o_=o_ap:
                            e.activation(out=o_, in_=bk[:, 0:512].rearrange(
                                "p (q s) -> p q s", q=4), func=AF.Identity), "sS")
                ev_Aev[(c, t, j)] = last
                bank_rel(b, "sS", last)
            P.wait("tensor", "sV", qv)
            loB = ((t + 1) % 2) * 4
            for j in range(4):
                b = loB + j
                bt = bank_acq(b)
                tv = None
                for qq in range(4):
                    g = 4 * j + qq
                    tv = P.op("tensor", lambda e, bk=bt, s_=qt[:, g * 128:(g + 1) * 128],
                              q_=qq: e.transpose(out=bk[:, q_ * 128:(q_ + 1) * 128],
                                                 in_=s_, identity=ident[:]),
                              "sT" if qq == 3 else None)
                ev_B[(c, t, j)] = tv
                P.wait("scalar", "sT", tv)
                last = None
                for qq in range(4):
                    g = 4 * j + qq
                    last = P.op("scalar", lambda e, bk=bt, q_=qq, g_=g,
                                o_=aT[:, g * CH + t * 128:g * CH + (t + 1) * 128]:
                                e.activation(out=o_, in_=bk[:, q_ * 128:(q_ + 1) * 128],
                                             func=AF.Sign, scale=pcol("gamma", g_),
                                             bias=pcol("beta", g_)),
                                "sS" if qq == 3 else None)
                ev_sign[(c, t, j)] = last
                bank_rel(b, "sS", last)
            xn_conds = [("sT", ev_A[(c, t, 3)])]
            q_conds = [("sT", ev_B[(c, t, 3)])]
            if dbg:
                xn_conds.append(dbg_xn_conds[(c, t)])
                q_conds.append(dbg_q_conds[(c, t)])
            xnr.set_free(c * TPC + t, xn_conds)
            qr.set_free(c * TPC + t, q_conds)

        ev_at_free_prev = None
        ev_sign_all = P.cnt["sS"]
        for kt2 in range(2):
            st_sem = f"dSpT{(c * 2 + kt2) % 2}"
            P.wait("gpsimd", "sS", ev_sign[(c, TPC - 1, 3)])
            dst = scat.ap()[kt2 * 8:(kt2 + 1) * 8, :, c * CH:(c + 1) * CH]
            dst = dst.rearrange("g p s -> p g s")
            spv = P.op("gpsimd", lambda e, d_=dst,
                       s_=aT[:, kt2 * 8 * CH:(kt2 + 1) * 8 * CH].rearrange(
                           "p (g s) -> p g s", g=8): e.dma_start(out=d_, in_=s_),
                       st_sem, 16)
            ev_spill[("at", c, kt2)] = (st_sem, spv)
        ev_at_free_prev = [ev_spill[("at", c, 0)], ev_spill[("at", c, 1)]]

        for ot_ in range(c * 3, min(c * 3 + 3, G)):
            emit_swgen_ot(ot_)
        if 1 <= c <= 4:
            convert_fam("out", "scalar", kts=range((c - 1) * 4, c * 4))

        def emit_wload(fam, j, c=c):
            cc, jj = c + j // 32, j % 32
            if cc >= nchunk:
                return
            mq_, kp = jj // 8, jj % 8
            if cc == 0 and jj == 0:
                P.wait("sync", "dCv_" + fam, ev_conv[fam])
            wt = wring[fam].acquire(P, "sync", cc * 32 + jj)
            lsem = f"dW_{fam}{(cc * 32 + jj) % 4}"
            src = wsc[fam].ap()[mq_, kp, :, :]
            wv = P.op("sync", lambda e, d_=wt[:], s_=src: e.dma_start(
                out=d_, in_=s_), lsem, 16)
            for kk in range(2):
                ev_wtile[(fam, cc, mq_, kp * 2 + kk)] = (lsem, wv, wt, cc * 32 + jj, kk)

        if c == 0:
            for fam in FAMS:
                for j in range(3):
                    emit_wload(fam, j)
        P.wait("tensor", "sS", ev_Aev[(c, TPC - 1, 3)])
        grpctr = 0
        for mq in range(4):
            for fam in FAMS:
                lo = (grpctr % 2) * 4
                grpctr += 1
                bts = [bank_acq(lo + m4) for m4 in range(4)]
                tv = None
                for kt in range(G):
                    lsem, wv, wt, wi, kk = ev_wtile[(fam, c, mq, kt)]
                    P.wait("tensor", lsem, wv)
                    for m4 in range(4):
                        tv = P.op("tensor", lambda e, bk=bts[m4],
                                  w_=wt[:, kk * CH + m4 * 128:kk * CH + (m4 + 1) * 128],
                                  r_=xnT[:, kt * CH:(kt + 1) * CH], k_=kt:
                                  e.matmul(out=bk[:, 0:512], lhsT=w_, rhs=r_,
                                           start=(k_ == 0), stop=(k_ == G - 1)),
                                  "sT" if m4 == 3 else None)
                    if kk == 1:
                        wring[fam].set_free(wi, [("sT", tv)])
                        emit_wload(fam, (mq * G + kt + 1) // 2 + 2)
                ev_D[(c, mq, fam)] = tv
                P.wait("scalar", "sT", tv)
                ring = {"in": rI, "dec": rD, "gate": rG}[fam]
                for m4 in range(4):
                    g = mq * 4 + m4
                    et = ring.acquire(P, "scalar", c * G + g)
                    evv = P.op("scalar", lambda e, bk=bts[m4], g_=g, o_=et[:], f_=fam:
                               e.activation(out=o_, in_=bk[:, 0:512], func=FAM_AF[f_],
                                            bias=pcol(FAM_BIAS[f_], g_)), "sS")
                    ev_evac[(c, fam, g)] = (evv, et)
                    bank_rel(lo + m4, "sS", evv)
                    if dbg:
                        dt_map = {"in": "dI", "dec": "dD", "gate": "dG"}
                        dbg_idg_conds[(c, fam, g)] = dbg_dump(
                            "sS", evv, et[:],
                            dbg_t[dt_map[fam]][g, :, c * CH:(c + 1) * CH])
            qg = []
            for m4 in range(4):
                g = mq * 4 + m4
                gidx = c * G + g
                iv, it_ = ev_evac[(c, "in", g)]
                dv, dt_ = ev_evac[(c, "dec", g)]
                gv, gt_ = ev_evac[(c, "gate", g)]
                ut = rU.acquire(P, "vector", gidx)
                P.wait("vector", "sS", max(iv, dv))
                uv = P.op("vector", lambda e, d_=dt_[:], i_=it_[:], o_=ut[:]:
                          e.scalar_tensor_tensor(out=o_, in0=d_, scalar=1.0, in1=i_,
                                                 op0=ALU.subtract, op1=ALU.mult), "sV")
                ev_u[(c, g)] = uv
                rI.set_free(gidx, [("sV", uv)] + ([dbg_idg_conds[(c, "in", g)]] if dbg else []))
                qg.append((g, gidx, ut, it_, dt_, gt_))
            scans = []
            for g, gidx, ut, it_, dt_, gt_ in qg:
                lt = rL.acquire(P, "vector", gidx)
                pt = rP.acquire(P, "vector", gidx)
                if c > 0:
                    P.wait("vector", "sS", ev_carry[(c - 1, g)])
                P.op("vector", lambda e, d_=dt_[:], u_=ut[:], o_=lt[:],
                     i_=bnd[:, g:g + 1]: e.tensor_tensor_scan(
                         out=o_, data0=d_, data1=u_, initial=i_,
                         op0=ALU.mult, op1=ALU.add))
                sv = P.op("vector", lambda e, d_=dt_[:], o_=pt[:],
                          i_=bnd[:, 16 + g:17 + g]: e.tensor_tensor_scan(
                              out=o_, data0=d_, data1=zeros[:], initial=i_,
                              op0=ALU.mult, op1=ALU.add), "sV")
                ev_scan[(c, g)] = sv
                rU.set_free(gidx, [("sV", sv)])
                scans.append((g, gidx, lt, pt, dt_, gt_, sv))
            for g, gidx, lt, pt, dt_, gt_, sv in scans:
                P.wait("scalar", "sV", sv)
                P.op("scalar", lambda e, l_=lt[:, CH - 1:CH], o_=bnd[:, g:g + 1]:
                     e.activation(out=o_, in_=l_, func=AF.Identity))
                cv = P.op("scalar", lambda e, p_=pt[:, CH - 1:CH],
                          o_=bnd[:, 16 + g:17 + g]: e.activation(
                              out=o_, in_=p_, func=AF.Identity), "sS")
                ev_carry[(c, g)] = cv
                dbg_lp = []
                if dbg:
                    dbg_lp = [dbg_dump("sV", sv, lt[:], dbg_t["dL"][g, :, c * CH:(c + 1) * CH]),
                              dbg_dump("sV", sv, pt[:], dbg_t["dP"][g, :, c * CH:(c + 1) * CH])]
                rD.set_free(gidx, [("sV", sv)] + ([dbg_idg_conds[(c, "dec", g)]] if dbg else []))
                at_ = rAs.acquire(P, "vector", gidx)
                bt_ = rBs.acquire(P, "vector", gidx)
                P.wait("vector", "sS", ev_evac[(c, "gate", g)][0])
                va_ = P.op("vector", lambda e, o_=at_[:], l_=lt[:], g2_=gt_[:]:
                           e.tensor_tensor(out=o_, in0=l_, in1=g2_, op=ALU.mult), "sV")
                vb_ = P.op("vector", lambda e, o_=bt_[:], p_=pt[:], g2_=gt_[:]:
                           e.tensor_tensor(out=o_, in0=p_, in1=g2_, op=ALU.mult), "sV")
                sa = f"dSpA{gidx % 4}"
                sb = f"dSpB{gidx % 4}"
                P.wait("gpsimd", "sV", vb_)
                spa = P.op("gpsimd", lambda e, s_=at_[:],
                           d_=sca.ap()[g, :, c * CH:(c + 1) * CH]:
                           e.dma_start(out=d_, in_=s_), sa, 16)
                spb = P.op("gpsimd", lambda e, s_=bt_[:],
                           d_=scb.ap()[g, :, c * CH:(c + 1) * CH]:
                           e.dma_start(out=d_, in_=s_), sb, 16)
                ev_spill[("a", c, g)] = (sa, spa)
                ev_spill[("b", c, g)] = (sb, spb)
                rAs.set_free(gidx, [(sa, spa)])
                rBs.set_free(gidx, [(sb, spb)])
                rG.set_free(gidx, [("sV", vb_)] + ([dbg_idg_conds[(c, "gate", g)]] if dbg else []))
                rL.set_free(gidx, [("sV", va_), ("sS", cv)] + dbg_lp[:1])
                rP.set_free(gidx, [("sV", vb_), ("sS", cv)] + dbg_lp[1:])

    P.wait("gpsimd", "sS", ev_carry[(nchunk - 1, G - 1)])
    agv = P.op("gpsimd", lambda e: e.dma_start(out=ag_in.ap()[:, :], in_=bnd[:]), "dAgIn", 16)
    P.wait("gpsimd", "dAgIn", agv)
    P.op("gpsimd", lambda e: e.collective_compute(
        "AllGather", ALU.bypass, replica_groups=[list(range(n_ranks))],
        ins=[ag_in.ap().opt()], outs=[ag_out.ap().opt()]).then_inc(P.hs["sCC"]))
    P.ops["gpsimd"].append(lambda e: e.wait_ge(P.hs["sCC"], 1))
    gl = None
    for j in range(n_ranks):
        gl = P.op("gpsimd", lambda e, o_=gath[:, j * 32:(j + 1) * 32],
                  s_=ag_out.ap()[j * 128:(j + 1) * 128, :]:
                  e.dma_start(out=o_, in_=s_), "dAgOut", 16)
    P.wait("vector", "dAgOut", gl)
    cur = 0
    for j in range(n_ranks):
        Lj = gath[:, j * 32:j * 32 + 16]
        Pj = gath[:, j * 32 + 16:j * 32 + 32]
        P.op("vector", lambda e, P_=Pj, s_=sinP[cur][:]: e.tensor_tensor(
            out=sinT[0][:], in0=P_, in1=s_, op=ALU.mult))
        P.op("vector", lambda e, L_=Lj: e.tensor_tensor(
            out=sinT[0][:], in0=sinT[0][:], in1=L_, op=ALU.add))
        P.op("vector", lambda e, s_=sinP[cur][:]: e.tensor_tensor(
            out=sinT[1][:], in0=sinT[0][:], in1=s_, op=ALU.subtract))
        P.op("vector", lambda e, m_=selm[:, j:j + 1], s_=sinP[cur][:],
             o_=sinP[1 - cur][:]: e.scalar_tensor_tensor(
                 out=o_, in0=sinT[1][:], scalar=m_, in1=s_,
                 op0=ALU.mult, op1=ALU.add))
        cur = 1 - cur
    sv_comb = P.op("vector", lambda e: e.nop(), "sV")
    P.wait("scalar", "sV", sv_comb)
    ev_sinF = P.op("scalar", lambda e, i_=sinP[cur][:]: e.activation(
        out=sinF[:], in_=i_, func=AF.Identity), "sS")

    ev_y = {}; ev_D2 = {}; ev_z = {}; ev_zT = {}; ev_fin = {}
    ev_w2 = {}; ev_x2 = {}; ev_abld = {}; ev_sw2 = {}; ev_atld = {}; ev_C2 = {}
    rGl2 = Ring(rL.tiles)
    rA2 = Ring(rA2.tiles + rD.tiles[0:2] + rP.tiles[0:2])
    rB2 = Ring(rB2.tiles + rD.tiles[2:4] + rP.tiles[2:4])
    last_D_all = ev_D[(nchunk - 1, 3, "gate")]
    def emit_x2load(i):
        if i < 0 or i >= NXT:
            return
        x2t = rX2.acquire(P, "gpsimd", NXT + i)
        lsem = f"dX{i % 2}"
        lv = P.op("gpsimd", lambda e, d_=x2t[:], s_=x_d[i * 128:(i + 1) * 128, :]:
                  e.dma_start(out=d_, in_=s_), lsem, 16)
        ev_x2[i] = (lsem, lv, x2t)

    def emit_w2load(j, c):
        cc, jj = c + j // 32, j % 32
        if cc >= nchunk:
            return
        mq_, kp = jj // 8, jj % 8
        if cc == 0 and jj == 0:
            P.wait("sync", "dCv_out", ev_conv["out"])
        wt = wring["out"].acquire(P, "sync", cc * 32 + jj)
        lsem = f"dW_out{(cc * 32 + jj) % 4}"
        src = wsc["out"].ap()[mq_, kp, :, :]
        wv = P.op("sync", lambda e, d_=wt[:], s_=src: e.dma_start(
            out=d_, in_=s_), lsem, 16)
        for kk in range(2):
            ev_w2[(cc, mq_, kp * 2 + kk)] = (lsem, wv, wt, cc * 32 + jj, kk)

    def emit_swload2(j, c):
        cc, jj = c + j // 32, j % 32
        if cc >= nchunk:
            return
        oq_, kp = jj // 8, jj % 8
        if cc == 0 and jj == 0:
            P.wait("sync", "dCv_sw", ev_conv["sw"])
        wt = swring.acquire(P, "sync", cc * 32 + jj)
        lsem = f"dWsw{(cc * 32 + jj) % 2}"
        src = swsc.ap()[oq_, kp, :, :]
        wv = P.op("sync", lambda e, d_=wt[:], s_=src: e.dma_start(
            out=d_, in_=s_), lsem, 16)
        for kk in range(2):
            ev_sw2[(cc, oq_, kp * 2 + kk)] = (lsem, wv, wt, cc * 32 + jj, kk)

    for j in range(2):
        emit_swload2(j, 0)
    for j in range(3):
        emit_w2load(j, 0)

    for s2 in ([f"dSpA{i}" for i in range(4)] + [f"dSpB{i}" for i in range(4)] +
               ["dSpT0", "dSpT1"]):
        P.wait("sync", s2, P.cnt[s2])
        P.wait("scalar", s2, P.cnt[s2])
        P.wait("gpsimd", s2, P.cnt[s2])

    emit_x2load(0)
    emit_x2load(1)

    def emit_abload_gi(gi):
        if gi >= nchunk * G:
            return
        c2, g = gi // G, gi % G
        at2 = rA2.acquire(P, "scalar", gi)
        bt2 = rB2.acquire(P, "gpsimd", gi)
        sa2 = f"dPA{gi % 6}"
        sb2 = f"dPB{gi % 6}"
        va = P.op("scalar", lambda e, d_=at2[:], s_=sca.ap()[g, :, c2 * CH:(c2 + 1) * CH]:
                  e.dma_start(out=d_, in_=s_), sa2, 16)
        vb = P.op("gpsimd", lambda e, d_=bt2[:], s_=scb.ap()[g, :, c2 * CH:(c2 + 1) * CH]:
                  e.dma_start(out=d_, in_=s_), sb2, 16)
        ev_abld[(c2, g)] = ((sa2, va), (sb2, vb), at2, bt2)

    for gi0 in range(6):
        emit_abload_gi(gi0)

    for c in range(nchunk):

        def emit_atload(c=c):
            if c > 0:
                P.wait("sync", "sT", ev_C2[(c - 1, 3)])
            vals = []
            for h in range(2):
                lsem = f"dPT{h}"
                src2 = scat.ap()[h * 8:(h + 1) * 8, :, c * CH:(c + 1) * CH]
                lv = P.op("sync", lambda e, d_=aT[:, h * 8 * CH:(h + 1) * 8 * CH].rearrange(
                    "p (g s) -> p g s", g=8), s_=src2.rearrange("g p s -> p g s"):
                    e.dma_start(out=d_, in_=s_), lsem, 16)
                vals.append((lsem, lv))
            return vals

        ev_atld[c] = emit_atload()
        P.wait("vector", "sS", ev_sinF)
        if c == 0:
            P.wait("vector", "sT", last_D_all)
            P.wait("vector", "dCv_out", ev_conv["out"])
        for g in range(G):
            (sa2, va), (sb2, vb), at2, bt2 = ev_abld[(c, g)]
            P.wait("vector", sa2, va)
            P.wait("vector", sb2, vb)
            if c > 0:
                P.wait("vector", "sT", ev_D2[(c - 1, 3)])
            yv = P.op("vector", lambda e, a_=at2[:], b_=bt2[:], m_=sinF[:, g:g + 1],
                      o_=xnT[:, g * CH:(g + 1) * CH]:
                      e.scalar_tensor_tensor(out=o_, in0=b_, scalar=m_, in1=a_,
                                             op0=ALU.mult, op1=ALU.add), "sV")
            ev_y[(c, g)] = yv
            rA2.set_free(c * G + g, [("sV", yv)])
            rB2.set_free(c * G + g, [("sV", yv)])
            emit_abload_gi(c * G + g + 6)

        for h2, (ls2, lv2) in enumerate(ev_atld[c]):
            P.wait("tensor", ls2, lv2)
        grp2 = 0
        for q in range(4):
            lo = (grp2 % 2) * 4
            grp2 += 1
            bts = [bank_acq(lo + m4) for m4 in range(4)]
            tv = None
            for kt in range(G):
                lsem, wv, wt, wi, kk = ev_sw2[(c, q, kt)]
                P.wait("tensor", lsem, wv)
                for m4 in range(4):
                    tv = P.op("tensor", lambda e, bk=bts[m4],
                              w_=wt[:, kk * CH + m4 * 128:kk * CH + (m4 + 1) * 128],
                              r_=aT[:, kt * CH:(kt + 1) * CH], k_=kt:
                              e.matmul(out=bk[:, 0:512], lhsT=w_, rhs=r_,
                                       start=(k_ == 0), stop=(k_ == G - 1)),
                              "sT" if m4 == 3 else None)
                if kk == 1:
                    swring.set_free(wi, [("sT", tv)])
                    emit_swload2((q * G + kt + 1) // 2 + 1, c)
            ev_C2[(c, q)] = tv
            P.wait("scalar", "sT", tv)
            gq = []
            for m4 in range(4):
                o = q * 4 + m4
                gt = rGl2.acquire(P, "scalar", c * G + o)
                gv = P.op("scalar", lambda e, bk=bts[m4], o_=o, g_=gt[:]: e.activation(
                    out=g_, in_=bk[:, 0:512], func=AF.Silu,
                    scale=pcol("alpha", o_), bias=pcol("alphabias", o_)), "sS")
                gq.append((gv, gt))
                bank_rel(lo + m4, "sS", gv)
            lo = (grp2 % 2) * 4
            grp2 += 1
            bts = [bank_acq(lo + m4) for m4 in range(4)]
            tv = None
            for kt in range(G):
                lsem, wv, wt, wi, kk = ev_w2[(c, q, kt)]
                P.wait("tensor", lsem, wv)
                P.wait("tensor", "sV", ev_y[(c, kt)])
                for m4 in range(4):
                    tv = P.op("tensor", lambda e, bk=bts[m4],
                              w_=wt[:, kk * CH + m4 * 128:kk * CH + (m4 + 1) * 128],
                              r_=xnT[:, kt * CH:(kt + 1) * CH], k_=kt:
                              e.matmul(out=bk[:, 0:512], lhsT=w_, rhs=r_,
                                       start=(k_ == 0), stop=(k_ == G - 1)),
                              "sT" if m4 == 3 else None)
                if kk == 1:
                    wring["out"].set_free(wi, [("sT", tv)])
                    emit_w2load((q * G + kt + 1) // 2 + 2, c)
            ev_D2[(c, q)] = tv
            P.wait("scalar", "sT", tv)
            for m4 in range(4):
                m = q * 4 + m4
                st_ = rSs.acquire(P, "scalar", c * G + m)
                sv2 = P.op("scalar", lambda e, bk=bts[m4], m_=m, o_=st_[:]:
                           e.activation(out=o_, in_=bk[:, 0:512], func=AF.Identity,
                                        scale=-1.0, bias=pcol("b_out", m_)), "sS")
                bank_rel(lo + m4, "sS", sv2)
                gv, gt2 = gq[m4]
                P.wait("vector", "sS", max(sv2, gv))
                if c > 0:
                    P.wait("vector", "sT", ev_zT[(c - 1, TPC - 1, 3)])
                zv = P.op("vector", lambda e, s_=st_[:], g_=gt2[:],
                          o_=zt[:, m * CH:(m + 1) * CH]: e.tensor_tensor(
                              out=o_, in0=s_, in1=g_, op=ALU.mult), "sV")
                ev_z[(c, m)] = zv
                rSs.set_free(c * G + m, [("sV", zv)])
                rGl2.set_free(c * G + m, [("sV", zv)])
        for t in range(TPC):
            ot = rOut.acquire(P, "vector", c * TPC + t)
            x2sem, lxv, x2t = ev_x2[c * TPC + t]
            for j in range(4):
                P.wait("tensor", "sV", ev_z[(c, 4 * j + 3)])
                b = (t % 2) * 4 + j
                bt = bank_acq(b)
                tv = None
                for qq in range(4):
                    g = 4 * j + qq
                    tv = P.op("tensor", lambda e, bk=bt, q_=qq,
                              s_=zt[:, g * CH + t * 128:g * CH + (t + 1) * 128]:
                              e.transpose(out=bk[:, q_ * 128:(q_ + 1) * 128],
                                          in_=s_, identity=ident[:]),
                              "sT" if qq == 3 else None)
                ev_zT[(c, t, j)] = tv
                P.wait("vector", "sT", tv)
                P.wait("vector", x2sem, lxv)
                fv = P.op("vector", lambda e, bk=bt, x_=x2t[:, j * 512:(j + 1) * 512],
                          o_=ot[:, j * 512:(j + 1) * 512]: e.tensor_tensor(
                              out=o_, in0=x_, in1=bk[:, 0:512], op=ALU.add), "sV")
                ev_fin[(c, t, j)] = fv
                bank_rel(b, "sV", fv)
            rX2.set_free(NXT + c * TPC + t, [("sV", ev_fin[(c, t, 3)])])
            emit_x2load(c * TPC + t + 2)
            P.wait("gpsimd", "sV", ev_fin[(c, t, 3)])
            idx = c * TPC + t
            osem = f"dOut{(c * TPC + t) % 2}"
            ov = P.op("gpsimd", lambda e, s_=ot[:], d_=out_d[idx * 128:(idx + 1) * 128, :]:
                      e.dma_start(out=d_, in_=s_), osem, 16)
            rOut.set_free(c * TPC + t, [(osem, ov)])

    P.wait("gpsimd", "dOut0", P.cnt["dOut0"])
    P.wait("gpsimd", "dOut1", P.cnt["dOut1"])

    with stack, nc.Block() as block:
        block.sync(lambda e: P.run("sync", e))
        block.scalar(lambda e: P.run("scalar", e))
        block.vector(lambda e: P.run("vector", e))
        block.tensor(lambda e: P.run("tensor", e))
        block.gpsimd(lambda e: P.run("gpsimd", e))
    nc._plan_trace = P.trace
    return nc


def _core_inputs(inputs, core, n_ranks=NCORE):
    b, q = core // 4, core % 4
    x = np.asarray(inputs["L0_attn_x"], np.float32)
    m = {
        "x": np.ascontiguousarray(x[b, q * 4096:(q + 1) * 4096, :]),
        "w_in": np.asarray(inputs["L0_attn_W_in"], np.float32),
        "w_dec": np.asarray(inputs["L0_attn_W_decay"], np.float32),
        "w_gate": np.asarray(inputs["L0_attn_W_gate"], np.float32),
        "w_out": np.asarray(inputs["L0_attn_W_out"], np.float32),
        "ffnw": np.asarray(inputs["L0_ffn_weight"], np.float32),
        "b_in": np.asarray(inputs["L0_attn_b_in"], np.float32),
        "b_dec": np.asarray(inputs["L0_attn_b_decay"], np.float32),
        "b_gate": np.asarray(inputs["L0_attn_b_gate"], np.float32),
        "b_out": np.asarray(inputs["L0_attn_b_out"], np.float32),
        "dbias": np.asarray(inputs["L0_attn_decay_bias"], np.float32),
        "gamma": np.asarray(inputs["L0_ffn_gamma"], np.float32),
        "beta": np.asarray(inputs["L0_ffn_beta"], np.float32),
        "ffnb": np.asarray(inputs["L0_ffn_bias"], np.float32),
        "alpha": np.asarray(inputs["L0_ffn_alpha"], np.float32),
        "ident": np.eye(128, dtype=np.float32),
    }
    sel = np.zeros(8, np.float32)
    for j in range(n_ranks):
        if j // 4 == b and j % 4 < q:
            sel[j] = 1.0
    m["selm"] = np.tile(sel[None, :], (128, 1)).astype(np.float32)
    return m


def kernel(**inputs):
    nc = build_program(nchunk=8, n_ranks=NCORE)
    in_maps = [_core_inputs(inputs, c) for c in range(NCORE)]
    res = run_bass_kernel_spmd(nc, in_maps, core_ids=list(range(NCORE)))
    out = np.zeros((2, 16384, 2048), np.float32)
    for c in range(NCORE):
        b, q = c // 4, c % 4
        out[b, q * 4096:(q + 1) * 4096, :] = res.results[c]["out"]
    return out
